# version 02
# baseline (speedup 1.0000x reference)
import contextlib
import numpy as np
import ml_dtypes

import concourse.bass as bass
import concourse.mybir as mybir
from concourse.bass_utils import run_bass_kernel_spmd

F32 = mybir.dt.float32
BF16 = mybir.dt.bfloat16
AF = mybir.ActivationFunctionType
ALU = mybir.AluOpType
AX = mybir.AxisListType.X

P = 128
S = 2048
E = 2048
H = 8192
NB = 512
SC = S // P
EC = E // P
HC = H // P
NBATCH = 2
GTOK = 512
NGRP = S // GTOK
GSTR = GTOK // P
BLK = P * E

EPS_LN = 1e-5
EPS_RMS = float(np.finfo(np.float32).eps)
NAT_LOG_EXP_SET = 6


class Ctr:
    def __init__(self, es, nc, name):
        self.h = es.enter_context(nc.semaphore(name))
        self.name = name
        self.n = 0

    def inc(self, instr, k=1):
        instr.then_inc(self.h, k)
        self.n += k
        return self.n


class MK:
    def __init__(self, debug=None, phases=None):
        self.debug = debug
        self.phases = phases
        nc = bass.Bass("TRN2", target_bir_lowering=False)
        self.nc = nc
        es = contextlib.ExitStack()
        self.es = es
        nc._mk_es = es

        d = nc.dram_tensor
        self.x = d("x", [NBATCH, S, E], F32, kind="ExternalInput").ap()
        self.wq = d("wq", [E, E], F32, kind="ExternalInput").ap()
        self.wk = d("wk", [E, E], F32, kind="ExternalInput").ap()
        self.wv = d("wv", [E, E], F32, kind="ExternalInput").ap()
        self.wo = d("wo", [E, E], F32, kind="ExternalInput").ap()
        self.wfc = d("wfc", [H, E], F32, kind="ExternalInput").ap()
        self.wproj = d("wproj", [E, H], F32, kind="ExternalInput").ap()
        self.ident_in = d("ident_in", [P, P], BF16, kind="ExternalInput").ap()
        self.ones_in = d("ones_in", [P, 1], BF16, kind="ExternalInput").ap()
        self.cbias_in = d("cbias_in", [P, 2], F32, kind="ExternalInput").ap()
        self.out = d("out", [NBATCH, S, E], F32, kind="ExternalOutput").ap()

        dk = "ExternalOutput" if debug else "Internal"
        self.qsl = d("qsl", [EC * BLK], BF16, kind=dk).ap()
        self.vfs = d("vfs", [EC * BLK], BF16, kind=dk).ap()
        self.ys = d("ys", [S * E], F32, kind=dk).ap()
        self.ln2d = d("ln2d", [EC * BLK], BF16, kind=dk).ap()
        self.w1q_d = d("w1q_d", [H * E], BF16, kind=dk).ap()
        self.w1qT_d = d("w1qT_d", [HC * BLK], BF16, kind=dk).ap()
        self.w2q_d = d("w2q_d", [E * H], BF16, kind=dk).ap()
        self.w2T_d = d("w2T_d", [HC * BLK], BF16, kind=dk).ap()
        self.rowd = d("rowd", [GTOK], F32, kind="Internal").ap()
        if debug:
            self.dbg0 = d("dbg0", [P, EC * E], BF16, kind="ExternalOutput").ap()
            self.dbg2 = d("dbg2", [P, EC * E], BF16, kind="ExternalOutput").ap()
            self.dbg3 = d("dbg3", [P, E], BF16, kind="ExternalOutput").ap()
            self.dbg4 = d("dbg4", [P, E], BF16, kind="ExternalOutput").ap()

        def sb(name, shape, dt):
            return es.enter_context(nc.sbuf_tensor(name, shape, dt))

        self.big0 = sb("big0", [P, EC * E], BF16)
        self.big2 = sb("big2", [P, EC * E], BF16)
        self.ident = sb("ident", [P, P], BF16)
        self.ones = sb("ones", [P, 1], BF16)
        self.rowrec = sb("rowrec", [P, SC], F32)
        for nm in ("negm", "rstd", "ssqa", "ssqb", "t1c", "mxc", "nmx8",
                   "srow", "negm2", "rstd2", "negm3", "lam", "tA", "tB"):
            setattr(self, nm, sb(nm, [P, 2], F32))
        self.lamcol = sb("lamcol", [P, GSTR], F32)
        self.cbias = sb("cbias", [P, 2], F32)
        self.ssqrow = sb("ssqrow", [1, GTOK], F32)
        self.lamrow = sb("lamrow", [1, GTOK], F32)

        names = ["sx0", "sx1", "swq", "swk", "swv0", "swv1", "swo",
                 "sqo0", "sqo1", "sqi0", "sqi1", "svo0", "svo1", "svi",
                 "syo", "sl2o", "sl2i", "sw1a", "sw1b", "sw2a", "sw2b",
                 "sql0", "sql1", "sqs0", "sqs1", "stri", "stro", "srb",
                 "sya", "sout", "sxc", "scst", "ts", "vs", "as_", "ps"]
        self.c = {n: Ctr(es, nc, n) for n in names}
        self.ev = {}
        self.lastw = {}
        self.blk = es.enter_context(nc.Block())

        self.quant_cursor = 0
        self.quant_loads_issued = 0
        self.transp1_cursor = 0
        self.transp2_cursor = 0

    def wt(self, eng, ename, ctr, val):
        if val is None or val <= 0:
            return
        lw = self.lastw.setdefault(ename, {})
        if lw.get(ctr.name, -1) >= val:
            return
        eng.wait_ge(ctr.h, val)
        lw[ctr.name] = val

    def wte(self, eng, ename, ctr, key):
        if key in self.ev:
            self.wt(eng, ename, ctr, self.ev[key])

    def col(self, t, k):
        return t[:, k:k + 1]

    def barrier_vals(self):
        return {n: self.c[n].n for n in ("ts", "vs", "as_", "ps")}

    def ldbar(self, eng, ename):
        for n in ("ts", "vs", "as_", "ps", "sqo0", "sqo1", "svo0", "svo1",
                  "syo", "sl2o", "sqs0", "sqs1", "sout", "sya", "sxc",
                  "stri", "stro"):
            self.wt(eng, ename, self.c[n], self.c[n].n)

    def build(self):
        self.preamble()
        stop = self.phases
        for b in range(NBATCH):
            bs = f"b{b}"
            self.phase_a(b)
            if stop in ("A", bs + "A"):
                break
            self.phase_a1b(b)
            if stop in ("K", bs + "K"):
                break
            self.phase_a2(b)
            if stop in ("V", bs + "V"):
                break
            self.phase_b1(b)
            if stop in ("B1", bs + "B1"):
                break
            self.phase_b2(b)
            if stop in ("B2", bs + "B2"):
                break
            self.phase_c(b)
            if stop in ("C", bs + "C"):
                break
            self.phase_d(b)
            if stop in ("D", "Dfc", "Dg0", "Dg1", "Dg2", bs + "D"):
                break
        self.finale()
        return self.nc

    def preamble(self):
        c = self.c

        @self.blk.scalar
        def _(s):
            s.add_instruction(
                mybir.InstLoadActFuncSet(
                    name=self.nc.get_next_instruction_name(),
                    act_func_set_id=NAT_LOG_EXP_SET,
                    ins=[], outs=[],
                )
            )

        @self.blk.sync
        def _(sy):
            c["scst"].inc(sy.dma_start(out=self.ident[:], in_=self.ident_in), 16)
            c["scst"].inc(sy.dma_start(out=self.ones[:], in_=self.ones_in), 16)
            c["scst"].inc(sy.dma_start(out=self.cbias[:], in_=self.cbias_in), 16)
        self.ev[("consts",)] = c["scst"].n

        @self.blk.gpsimd
        def _(g):
            for cch in range(EC):
                ins = g.dma_start(
                    out=self.big2[:, cch * E:(cch + 1) * E],
                    in_=self.wq[cch * P:(cch + 1) * P, :])
                c["swq"].inc(ins, 16)
        self.ev[("wqld", 0)] = c["swq"].n

        self.qscope = contextlib.ExitStack()
        qsb = lambda n, s, dt: self.qscope.enter_context(
            self.nc.sbuf_tensor(n, s, dt))
        self.q_wb = [qsb(f"q_wb{k}", [P, E], BF16) for k in range(2)]
        self.q_quot = [qsb(f"q_quot{k}", [P, E], BF16) for k in range(2)]
        self.q_sgn = [qsb(f"q_sgn{k}", [P, E], BF16) for k in range(2)]
        self.q_sums = [qsb(f"q_sums{k}", [P, E // 64], F32) for k in range(2)]
        self.q_scl = [qsb(f"q_scl{k}", [P, E // 64], BF16) for k in range(2)]
        self.q_thr = [qsb(f"q_thr{k}", [P, E // 64], F32) for k in range(2)]
        self.trscope = contextlib.ExitStack()
        self.trbuf = self.trscope.enter_context(
            self.nc.sbuf_tensor("trbuf", [P, E], BF16))

    def quant_unit_src(self, u):
        if u < 64:
            src = bass.AP(self.wfc.tensor, u * P * E, [[E, P], [1, E]])
            dst = bass.AP(self.w1q_d.tensor, u * P * E, [[E, P], [1, E]])
        else:
            v = u - 64
            dc, qq = v // 4, v % 4
            off = dc * P * H + qq * E
            src = bass.AP(self.wproj.tensor, off, [[H, P], [1, E]])
            dst = bass.AP(self.w2q_d.tensor, off, [[H, P], [1, E]])
        return src, dst

    def quant_issue_load(self):
        u = self.quant_loads_issued
        if u >= 128:
            return
        self.quant_loads_issued += 1
        c = self.c
        src, _ = self.quant_unit_src(u)

        @self.blk.gpsimd
        def _(g):
            self.wte(g, "gpsimd", c["vs"], ("qw", u - 2))
            self.wte(g, "gpsimd", c[f"sqs{u % 2}"], ("qst", u - 2))
            c[f"sql{u % 2}"].inc(
                g.dma_start(out=self.q_wb[u % 2][:], in_=src), 16)
        self.ev[("qld", u)] = c[f"sql{u % 2}"].n

    def quant_step(self, n):
        for _ in range(n):
            u = self.quant_cursor
            if u >= 128:
                return
            self.quant_cursor += 1
            while self.quant_loads_issued < min(u + 2, 128):
                self.quant_issue_load()
            c = self.c
            k = u % 2
            wb = self.q_wb[k]
            absq = self.q_quot[k]
            sgn = self.q_sgn[k]
            sums, scl, thr = self.q_sums[k], self.q_scl[k], self.q_thr[k]

            @self.blk.scalar
            def _(s):
                self.wte(s, "scalar", c[f"sql{k}"], ("qld", u))
                self.wte(s, "scalar", c["vs"], ("qw", u - 2))
                c["as_"].inc(s.activation(out=absq[:], in_=wb[:], func=AF.Abs))
                self.wt(s, "scalar", c["as_"], c["as_"].n)
                c["as_"].inc(s.activation(out=sgn[:], in_=wb[:], func=AF.Sign))
            self.ev[("qact", u)] = c["as_"].n

            @self.blk.vector
            def _(v):
                sv = lambda: self.wt(v, "vector", c["vs"], c["vs"].n)
                self.wte(v, "vector", c[f"sql{k}"], ("qld", u))
                c["vs"].inc(v.tensor_reduce(
                    sums[:], wb[:].rearrange("p (g k) -> p g k", k=64),
                    axis=AX, op=ALU.add, apply_absolute_value=True))
                sv()
                c["vs"].inc(v.tensor_scalar(
                    scl[:], sums[:], 1.0 / 64, 1e-8,
                    op0=ALU.mult, op1=ALU.max))
                sv()
                c["vs"].inc(v.tensor_scalar_mul(thr[:], scl[:], 0.5))
                sv()
                self.wte(v, "vector", c["as_"], ("qact", u))
                thr_bc = bass.AP(thr, 0, [[E // 64, P], [1, E // 64], [0, 64]])
                c["vs"].inc(v.tensor_tensor(
                    absq[:].rearrange("p (g k) -> p g k", k=64),
                    absq[:].rearrange("p (g k) -> p g k", k=64),
                    thr_bc, op=ALU.is_gt))
                sv()
                scl_bc = bass.AP(scl, 0, [[E // 64, P], [1, E // 64], [0, 64]])
                c["vs"].inc(v.tensor_tensor(
                    sgn[:].rearrange("p (g k) -> p g k", k=64),
                    sgn[:].rearrange("p (g k) -> p g k", k=64),
                    scl_bc, op=ALU.mult))
                sv()
                c["vs"].inc(v.tensor_mul(wb[:], absq[:], sgn[:]))
            self.ev[("qw", u)] = c["vs"].n

            @self.blk.sync
            def _(sy):
                self.wte(sy, "sync", c["vs"], ("qw", u))
                _, dst = self.quant_unit_src(u)
                c[f"sqs{k}"].inc(sy.dma_start(out=dst, in_=wb[:]), 16)
            self.ev[("qst", u)] = c[f"sqs{k}"].n
            while self.quant_loads_issued < min(u + 3, 128):
                self.quant_issue_load()
            if u == 63:
                self.ev[("qfc_done",)] = (c["sqs0"].n, c["sqs1"].n)
            if u == 127:
                self.ev[("qpr_done",)] = (c["sqs0"].n, c["sqs1"].n)

    def transp1_step(self, n):
        if ("qfc_done",) not in self.ev:
            return
        for _ in range(n):
            u = self.transp1_cursor
            if u >= EC * 4:
                return
            self.transp1_cursor += 1
            dc, q = u // 4, u % 4
            c = self.c

            @self.blk.sync
            def _(sy):
                v0, v1 = self.ev[("qfc_done",)]
                self.wt(sy, "sync", c["sqs0"], v0)
                self.wt(sy, "sync", c["sqs1"], v1)
                self.wte(sy, "sync", c["stro"], ("t1o", u - 1))
                src = bass.AP(self.w1q_d.tensor, q * 2048 * E + dc * P,
                              [[E, 2048], [1, P]])
                c["stri"].inc(
                    sy.dma_start(out=self.trbuf[:], in_=src, transpose=True),
                    16)
            self.ev[("t1i", u)] = c["stri"].n

            @self.blk.gpsimd
            def _(g):
                self.wte(g, "gpsimd", c["stri"], ("t1i", u))
                dst = bass.AP(
                    self.w1qT_d.tensor,
                    q * 16 * BLK + dc * P,
                    [[E, P], [BLK, 16], [1, P]])
                c["stro"].inc(g.dma_start(
                    out=dst,
                    in_=self.trbuf[:].rearrange("p (j i) -> p j i", i=P)), 16)
            self.ev[("t1o", u)] = c["stro"].n
            if u == EC * 4 - 1:
                self.ev[("w1T_done",)] = c["stro"].n

    def transp2_step(self, n):
        if ("qpr_done",) not in self.ev:
            return
        for _ in range(n):
            jc = self.transp2_cursor
            if jc >= HC:
                return
            self.transp2_cursor += 1
            c = self.c

            @self.blk.sync
            def _(sy):
                v0, v1 = self.ev[("qpr_done",)]
                self.wt(sy, "sync", c["sqs0"], v0)
                self.wt(sy, "sync", c["sqs1"], v1)
                self.wte(sy, "sync", c["stro"], ("t2o", jc - 1))
                if jc == 0:
                    self.wte(sy, "sync", c["stro"], ("w1T_done",))
                src = bass.AP(self.w2q_d.tensor, jc * P, [[H, E], [1, P]])
                c["stri"].inc(sy.dma_start(
                    out=self.trbuf[:, 0:E], in_=src, transpose=True), 16)
            self.ev[("t2i", jc)] = c["stri"].n

            @self.blk.gpsimd
            def _(g):
                self.wte(g, "gpsimd", c["stri"], ("t2i", jc))
                dst = bass.AP(self.w2T_d.tensor, jc * BLK, [[E, P], [1, E]])
                c["stro"].inc(g.dma_start(out=dst, in_=self.trbuf[:, 0:E]),
                              16)
            self.ev[("t2o", jc)] = c["stro"].n
            if jc == HC - 1:
                self.ev[("w2T_done",)] = c["stro"].n

    def phase_a(self, b):
        c, nc = self.c, self.nc
        sc = contextlib.ExitStack()
        ssb = lambda n, s, dt: sc.enter_context(nc.sbuf_tensor(n, s, dt))
        sps = lambda n, s, dt: sc.enter_context(nc.psum_tensor(n, s, dt))
        xst = [ssb(f"a{b}_xst{k}", [P, E], F32) for k in range(2)]
        ln1s = [ssb(f"a{b}_ln1s{k}", [P, E], BF16) for k in range(2)]
        qstage = [ssb(f"a{b}_qst{k}", [P, E], BF16) for k in range(2)]
        psQ = sps(f"a{b}_psQ", [P, E], F32)
        psT = sps(f"a{b}_psT", [P, E], BF16)
        bar = self.barrier_vals()

        @self.blk.sync
        def _(sy):
            for n, v in bar.items():
                self.wt(sy, "sync", c[n], v)
            self.wte(sy, "sync", c["sout"], ("outdone", b - 1))
            for i in (0, 1):
                ins = sy.dma_start(
                    out=xst[i % 2][:], in_=self.x[b, i * P:(i + 1) * P, :])
                c[f"sx{i % 2}"].inc(ins, 16)
                self.ev[("xld", b, i)] = c[f"sx{i % 2}"].n

        for i in range(SC + 1):
            if i < SC:
                @self.blk.vector
                def _(v):
                    self.wte(v, "vector", c[f"sx{i % 2}"], ("xld", b, i))
                    self.wte(v, "vector", c["as_"], ("sq", b, i - 2))
                    c["vs"].inc(v.reduce_sum(
                        self.col(self.tA, i % 2), xst[i % 2][:], axis=AX))
                    self.wt(v, "vector", c["vs"], c["vs"].n)
                    c["vs"].inc(v.tensor_scalar_mul(
                        self.col(self.negm, i % 2),
                        self.col(self.tA, i % 2), -1.0 / E))
                self.ev[("negm", b, i)] = c["vs"].n

            @self.blk.scalar
            def _(s):
                if i - 1 >= 0:
                    self.wte(s, "scalar", c["ts"], ("T", b, i - 1))
                    dst = bass.AP(self.big0, (i - 1) * P,
                                  [[EC * E, P], [E, EC], [1, P]])
                    c["as_"].inc(s.copy(
                        dst, psT[:].rearrange("p (t r) -> p t r", r=P)))
                    self.ev[("evT", b, i - 1)] = c["as_"].n
                if i < SC:
                    self.wte(s, "scalar", c["scst"], ("consts",))
                    self.wte(s, "scalar", c["vs"], ("negm", b, i))
                    self.wte(s, "scalar", c["ts"], ("T", b, i - 2))
                    c["as_"].inc(s.activation(
                        out=ln1s[i % 2][:], in_=xst[i % 2][:], func=AF.Square,
                        bias=self.col(self.negm, i % 2),
                        accum_out=self.col(self.ssqa, i % 2)))
                    self.ev[("sq", b, i)] = c["as_"].n
                    self.wt(s, "scalar", c["as_"], c["as_"].n)
                    c["as_"].inc(s.activation(
                        out=self.col(self.t1c, i % 2),
                        in_=self.col(self.ssqa, i % 2), func=AF.Ln,
                        scale=1.0 / E, bias=self.col(self.cbias, 0)))
                    self.wt(s, "scalar", c["as_"], c["as_"].n)
                    c["as_"].inc(s.activation(
                        out=self.col(self.rstd, i % 2),
                        in_=self.col(self.t1c, i % 2), func=AF.Exp,
                        scale=-0.5))
                    self.ev[("rstd", b, i)] = c["as_"].n

            if i < SC:
                @self.blk.vector
                def _(v):
                    self.wte(v, "vector", c["as_"], ("rstd", b, i))
                    c["vs"].inc(v.tensor_scalar(
                        ln1s[i % 2][:], xst[i % 2][:],
                        self.col(self.negm, i % 2),
                        self.col(self.rstd, i % 2),
                        op0=ALU.add, op1=ALU.mult))
                self.ev[("ln1s", b, i)] = c["vs"].n

            @self.blk.tensor
            def _(t):
                if i - 1 >= 0:
                    j = i - 1
                    self.wte(t, "tensor", c["as_"], ("evT", b, j))
                    self.wte(t, "tensor", c["vs"], ("evQ", b, j - 1))
                    self.wte(t, "tensor", c["swq"], ("wqld", b))
                    for e in range(EC):
                        for sbk in range(4):
                            mm = t.matmul(
                                psQ[:, sbk * NB:(sbk + 1) * NB],
                                self.big0[:, e * E + j * P:
                                          e * E + j * P + P],
                                self.big2[:, e * E + sbk * NB:
                                          e * E + (sbk + 1) * NB],
                                start=(e == 0), stop=(e == EC - 1))
                    c["ts"].inc(mm)
                    self.ev[("Q", b, j)] = c["ts"].n
                if i < SC:
                    self.wte(t, "tensor", c["scst"], ("consts",))
                    self.wte(t, "tensor", c["vs"], ("ln1s", b, i))
                    self.wte(t, "tensor", c["as_"], ("evT", b, i - 1))
                    for tt in range(EC):
                        mm = t.transpose(
                            psT[:, tt * P:(tt + 1) * P],
                            ln1s[i % 2][:, tt * P:(tt + 1) * P],
                            self.ident[:])
                    c["ts"].inc(mm)
                    self.ev[("T", b, i)] = c["ts"].n

            if i - 1 >= 0:
                j = i - 1

                @self.blk.vector
                def _(v):
                    self.wte(v, "vector", c["ts"], ("Q", b, j))
                    self.wte(v, "vector", c[f"sqo{j % 2}"], ("qst", b, j - 2))
                    c["vs"].inc(v.tensor_copy(qstage[j % 2][:], psQ[:]))
                self.ev[("evQ", b, j)] = c["vs"].n

                @self.blk.gpsimd
                def _(g):
                    self.wte(g, "gpsimd", c["vs"], ("evQ", b, j))
                    dst = bass.AP(self.qsl.tensor, j * P,
                                  [[E, P], [BLK, EC], [1, P]])
                    c[f"sqo{j % 2}"].inc(g.dma_start(
                        out=dst,
                        in_=qstage[j % 2][:].rearrange(
                            "p (a i) -> p a i", i=P)), 16)
                self.ev[("qst", b, j)] = c[f"sqo{j % 2}"].n

            if i + 2 < SC:
                @self.blk.sync
                def _(sy):
                    self.wte(sy, "sync", c["as_"], ("sq", b, i))
                    self.wte(sy, "sync", c["vs"], ("ln1s", b, i))
                    c[f"sx{i % 2}"].inc(sy.dma_start(
                        out=xst[i % 2][:],
                        in_=self.x[b, (i + 2) * P:(i + 3) * P, :]), 16)
                self.ev[("xld", b, i + 2)] = c[f"sx{i % 2}"].n

            if b == 0:
                self.quant_step(1)
        sc.close()

    def phase_a1b(self, b):
        c, nc = self.c, self.nc
        sc = contextlib.ExitStack()
        wkh = sc.enter_context(
            nc.sbuf_tensor(f"k{b}_wkh", [P, EC * 1024], BF16))
        psK = [sc.enter_context(
            nc.psum_tensor(f"k{b}_psK{k}", [P, 1024], F32)) for k in range(2)]

        for h in range(2):
            @self.blk.gpsimd
            def _(g):
                if h == 0:
                    self.ldbar(g, "gpsimd")
                if h == 1:
                    self.wte(g, "gpsimd", c["ts"], ("K", b, 0, SC - 1))
                for cc in range(EC):
                    src = bass.AP(self.wk.tensor, cc * P * E + h * 1024,
                                  [[E, P], [1, 1024]])
                    c["swk"].inc(g.dma_start(
                        out=wkh[:, cc * 1024:(cc + 1) * 1024], in_=src), 16)
                    self.ev[("wkld", b, h, cc)] = c["swk"].n

            for i in range(SC):
                @self.blk.tensor
                def _(t):
                    self.wte(t, "tensor", c["swk"], ("wkld", b, h, EC - 1))
                    if i >= 2:
                        self.wte(t, "tensor", c["vs"], ("evK", b, h, i - 2))
                    elif h == 1:
                        self.wte(t, "tensor", c["vs"],
                                 ("evK", b, 0, SC + i - 2))
                    if h == 0 and i == 0:
                        self.wte(t, "tensor", c["as_"], ("evT", b, SC - 1))
                        self.wte(t, "tensor", c["ts"], ("Q", b, SC - 1))
                        self.wte(t, "tensor", c["vs"], ("evQ", b, SC - 1))
                    for cc in range(EC):
                        for sbk in range(2):
                            mm = t.matmul(
                                psK[i % 2][:, sbk * NB:(sbk + 1) * NB],
                                self.big0[:, cc * E + i * P:
                                          cc * E + i * P + P],
                                wkh[:, cc * 1024 + sbk * NB:
                                    cc * 1024 + (sbk + 1) * NB],
                                start=(cc == 0), stop=(cc == EC - 1))
                    c["ts"].inc(mm)
                    self.ev[("K", b, h, i)] = c["ts"].n

                @self.blk.vector
                def _(v):
                    self.wte(v, "vector", c["ts"], ("K", b, h, i))
                    c["vs"].inc(v.tensor_copy(
                        self.big2[:, i * E + h * 1024:
                                  i * E + h * 1024 + 1024],
                        psK[i % 2][:]))
                self.ev[("evK", b, h, i)] = c["vs"].n

                if b == 0:
                    self.quant_step(1)
        sc.close()

    def phase_a2(self, b):
        c, nc = self.c, self.nc
        sc = contextlib.ExitStack()
        wvb = [sc.enter_context(
            nc.sbuf_tensor(f"v{b}_wvb{k}", [P, E], BF16)) for k in range(2)]
        vstage = [sc.enter_context(
            nc.sbuf_tensor(f"v{b}_vst{k}", [P, E], BF16)) for k in range(2)]
        psV = [sc.enter_context(
            nc.psum_tensor(f"v{b}_psV{k}", [P, E], F32)) for k in range(2)]

        def wv_load(fc):
            @self.blk.gpsimd
            def _(g):
                if fc < 2:
                    self.ldbar(g, "gpsimd")
                self.wte(g, "gpsimd", c["ts"], ("V", b, fc - 2))
                src = bass.AP(self.wv.tensor, fc * P,
                              [[E, P], [P * E, EC], [1, P]])
                c[f"swv{fc % 2}"].inc(g.dma_start(out=wvb[fc % 2][:],
                                                  in_=src), 16)
            self.ev[("wvld", b, fc)] = c[f"swv{fc % 2}"].n

        wv_load(0)
        wv_load(1)
        for fc in range(EC):
            @self.blk.tensor
            def _(t):
                self.wte(t, "tensor", c[f"swv{fc % 2}"], ("wvld", b, fc))
                self.wte(t, "tensor", c["vs"], ("evV", b, fc - 2))
                if fc == 0:
                    self.wte(t, "tensor", c["vs"], ("evK", b, 1, SC - 1))
                for tt in range(EC):
                    for sbk in range(4):
                        mm = t.matmul(
                            psV[fc % 2][:, sbk * NB:(sbk + 1) * NB],
                            wvb[fc % 2][:, tt * P:(tt + 1) * P],
                            self.big0[:, tt * E + sbk * NB:
                                      tt * E + (sbk + 1) * NB],
                            start=(tt == 0), stop=(tt == EC - 1))
                c["ts"].inc(mm)
                self.ev[("V", b, fc)] = c["ts"].n

            @self.blk.vector
            def _(v):
                self.wte(v, "vector", c["ts"], ("V", b, fc))
                self.wte(v, "vector", c[f"svo{fc % 2}"], ("vst", b, fc - 2))
                c["vs"].inc(v.tensor_copy(vstage[fc % 2][:], psV[fc % 2][:]))
            self.ev[("evV", b, fc)] = c["vs"].n

            @self.blk.gpsimd
            def _(g):
                self.wte(g, "gpsimd", c["vs"], ("evV", b, fc))
                dst = bass.AP(self.vfs.tensor, fc * BLK, [[E, P], [1, E]])
                c[f"svo{fc % 2}"].inc(g.dma_start(out=dst,
                                                  in_=vstage[fc % 2][:]), 16)
            self.ev[("vst", b, fc)] = c[f"svo{fc % 2}"].n

            if fc + 2 < EC:
                wv_load(fc + 2)
            if b == 0:
                self.quant_step(3)
                self.transp1_step(4)
        if b == 0:
            self.quant_step(128)
            self.transp1_step(EC * 4)
        sc.close()

    def phase_b1(self, b):
        c, nc = self.c, self.nc
        sc = contextlib.ExitStack()
        qin = [sc.enter_context(
            nc.sbuf_tensor(f"b1{b}_qin{k}", [P, E], BF16)) for k in range(2)]
        psb = [sc.enter_context(
            nc.sbuf_tensor(f"b1{b}_psb{k}", [P, E], BF16)) for k in range(2)]
        psS = sc.enter_context(nc.psum_tensor(f"b1{b}_psS", [P, E], F32))
        psTb = sc.enter_context(nc.psum_tensor(f"b1{b}_psTb", [P, E], BF16))

        def q_load(j):
            @self.blk.sync
            def _(sy):
                if j < 2:
                    self.ldbar(sy, "sync")
                self.wt(sy, "sync", c["sqo0"], c["sqo0"].n)
                self.wt(sy, "sync", c["sqo1"], c["sqo1"].n)
                self.wte(sy, "sync", c["ts"], ("S", b, j - 2))
                src = bass.AP(self.qsl.tensor, j * BLK, [[E, P], [1, E]])
                c[f"sqi{j % 2}"].inc(sy.dma_start(out=qin[j % 2][:],
                                                  in_=src), 16)
            self.ev[("qld2", b, j)] = c[f"sqi{j % 2}"].n

        q_load(0)
        q_load(1)
        for j in range(SC + 1):
            @self.blk.tensor
            def _(t):
                if j < SC:
                    self.wte(t, "tensor", c[f"sqi{j % 2}"], ("qld2", b, j))
                    self.wte(t, "tensor", c["as_"], ("Exp", b, j - 1))
                    if j == 0:
                        self.wte(t, "tensor", c["vs"],
                                 ("evK", b, 1, SC - 1))
                        self.wte(t, "tensor", c["vs"], ("evV", b, EC - 1))
                    for cc in range(SC):
                        for sbk in range(4):
                            mm = t.matmul(
                                psS[:, sbk * NB:(sbk + 1) * NB],
                                qin[j % 2][:, cc * P:(cc + 1) * P],
                                self.big2[:, cc * E + sbk * NB:
                                          cc * E + (sbk + 1) * NB],
                                start=(cc == 0), stop=(cc == SC - 1))
                    c["ts"].inc(mm)
                    self.ev[("S", b, j)] = c["ts"].n
                if j - 1 >= 0:
                    jj = j - 1
                    self.wte(t, "tensor", c["as_"], ("Exp", b, jj))
                    self.wte(t, "tensor", c["as_"], ("evTb", b, jj - 1))
                    for tt in range(EC):
                        mm = t.transpose(
                            psTb[:, tt * P:(tt + 1) * P],
                            psb[jj % 2][:, tt * P:(tt + 1) * P],
                            self.ident[:])
                    c["ts"].inc(mm)
                    self.ev[("Tb", b, jj)] = c["ts"].n

            if j < SC:
                @self.blk.vector
                def _(v):
                    self.wte(v, "vector", c["ts"], ("S", b, j))
                    self.wte(v, "vector", c["as_"], ("Exp", b, j - 2))
                    c["vs"].inc(v.reduce_max(
                        self.col(self.mxc, j % 2), psS[:], axis=AX))
                    self.wt(v, "vector", c["vs"], c["vs"].n)
                    c["vs"].inc(v.tensor_scalar_mul(
                        self.col(self.nmx8, j % 2),
                        self.col(self.mxc, j % 2), -0.125))
                self.ev[("nmx8", b, j)] = c["vs"].n

                @self.blk.scalar
                def _(s):
                    self.wte(s, "scalar", c["vs"], ("nmx8", b, j))
                    self.wte(s, "scalar", c["ts"], ("Tb", b, j - 2))
                    c["as_"].inc(s.activation(
                        out=psb[j % 2][:], in_=psS[:], func=AF.Exp,
                        scale=0.125, bias=self.col(self.nmx8, j % 2),
                        accum_out=self.col(self.srow, j % 2)))
                self.ev[("Exp", b, j)] = c["as_"].n

                @self.blk.vector
                def _(v):
                    self.wte(v, "vector", c["as_"], ("Exp", b, j))
                    c["vs"].inc(v.reciprocal(
                        self.rowrec[:, j:j + 1], self.col(self.srow, j % 2)))
                self.ev[("rr", b, j)] = c["vs"].n

                if self.debug and b == 1 and j == 0:
                    @self.blk.gpsimd
                    def _(g):
                        self.wte(g, "gpsimd", c["as_"], ("Exp", b, j))
                        c["sout"].inc(g.dma_start(
                            out=self.dbg3, in_=psb[0][:]), 16)
                        self.wte(g, "gpsimd", c[f"sqi{0}"], ("qld2", b, 0))
                        c["sout"].inc(g.dma_start(
                            out=self.dbg4, in_=qin[0][:]), 16)

            if j - 1 >= 0:
                jj = j - 1

                @self.blk.scalar
                def _(s):
                    self.wte(s, "scalar", c["ts"], ("Tb", b, jj))
                    c["as_"].inc(s.copy(
                        self.big0[:, jj * E:(jj + 1) * E], psTb[:]))
                self.ev[("evTb", b, jj)] = c["as_"].n

            if j + 2 < SC:
                q_load(j + 2)
            if b == 0:
                self.quant_step(2)
                self.transp2_step(2)
        sc.close()

    def phase_b2(self, b):
        c, nc = self.c, self.nc
        sc = contextlib.ExitStack()
        psO = [sc.enter_context(
            nc.psum_tensor(f"b2{b}_psO{k}", [P, E], F32)) for k in range(2)]

        @self.blk.sync
        def _(sy):
            self.wte(sy, "sync", c["ts"], ("S", b, SC - 1))
            self.wt(sy, "sync", c["svo0"], c["svo0"].n)
            self.wt(sy, "sync", c["svo1"], c["svo1"].n)
            for cc in range(EC):
                src = bass.AP(self.vfs.tensor, cc * BLK, [[E, P], [1, E]])
                c["svi"].inc(sy.dma_start(
                    out=self.big2[:, cc * E:(cc + 1) * E], in_=src), 16)
        self.ev[("vfall", b)] = c["svi"].n

        for j in range(SC):
            @self.blk.tensor
            def _(t):
                self.wte(t, "tensor", c["vs"], ("evO", b, j - 2))
                if j == 0:
                    self.wte(t, "tensor", c["as_"], ("evTb", b, SC - 1))
                    self.wte(t, "tensor", c["vs"], ("rr", b, SC - 1))
                self.wte(t, "tensor", c["svi"], ("vfall", b))
                for tt in range(EC):
                    for sbk in range(4):
                        mm = t.matmul(
                            psO[j % 2][:, sbk * NB:(sbk + 1) * NB],
                            self.big0[:, j * E + tt * P: j * E + tt * P + P],
                            self.big2[:, tt * E + sbk * NB:
                                      tt * E + (sbk + 1) * NB],
                            start=(tt == 0), stop=(tt == EC - 1))
                c["ts"].inc(mm)
                self.ev[("O", b, j)] = c["ts"].n

            @self.blk.vector
            def _(v):
                self.wte(v, "vector", c["ts"], ("O", b, j))
                c["vs"].inc(v.tensor_single_scalar(
                    self.big0[:, j * E:(j + 1) * E], psO[j % 2][:],
                    self.rowrec[:, j:j + 1], op=ALU.mult))
            self.ev[("evO", b, j)] = c["vs"].n

            if b == 0:
                self.quant_step(2)
                self.transp2_step(2)
        if b == 0:
            self.quant_step(128)
            self.transp2_step(HC)

        @self.blk.gpsimd
        def _(g):
            self.wte(g, "gpsimd", c["ts"], ("O", b, SC - 1))
            if b == 0:
                self.wte(g, "gpsimd", c["stro"], ("w2T_done",))
            for cc in range(EC):
                ins = g.dma_start(
                    out=self.big2[:, cc * E:(cc + 1) * E],
                    in_=self.wo[cc * P:(cc + 1) * P, :])
                c["swo"].inc(ins, 16)
        self.ev[("woall", b)] = c["swo"].n
        sc.close()
        if b == 0:
            self.trscope.close()
            self.qscope.close()

    def phase_c(self, b):
        c, nc = self.c, self.nc
        sc = contextlib.ExitStack()
        ystage = sc.enter_context(nc.sbuf_tensor(f"c{b}_yst", [P, E], F32))
        ln2s = [sc.enter_context(
            nc.sbuf_tensor(f"c{b}_ln2s{k}", [P, E], BF16)) for k in range(2)]
        l2stage = sc.enter_context(nc.sbuf_tensor(f"c{b}_l2st", [P, E], BF16))
        psC = sc.enter_context(nc.psum_tensor(f"c{b}_psC", [P, E], F32))
        psTc = sc.enter_context(nc.psum_tensor(f"c{b}_psTc", [P, E], BF16))

        for i in range(SC + 1):
            @self.blk.tensor
            def _(t):
                if i < SC:
                    self.wte(t, "tensor", c["vs"], ("ap1", b, i - 1))
                    if i == 0:
                        self.wte(t, "tensor", c["vs"], ("evO", b, SC - 1))
                    self.wte(t, "tensor", c["swo"], ("woall", b))
                    for tt in range(EC):
                        for sbk in range(4):
                            mm = t.matmul(
                                psC[:, sbk * NB:(sbk + 1) * NB],
                                self.big0[:, tt * E + i * P:
                                          tt * E + i * P + P],
                                self.big2[:, tt * E + sbk * NB:
                                          tt * E + (sbk + 1) * NB],
                                start=(tt == 0), stop=(tt == EC - 1))
                    c["ts"].inc(mm)
                    self.ev[("PRJ", b, i)] = c["ts"].n
                if i - 1 >= 0:
                    ii = i - 1
                    self.wte(t, "tensor", c["vs"], ("ln2s", b, ii))
                    self.wte(t, "tensor", c["as_"], ("evTc", b, ii - 1))
                    for tt in range(EC):
                        mm = t.transpose(
                            psTc[:, tt * P:(tt + 1) * P],
                            ln2s[ii % 2][:, tt * P:(tt + 1) * P],
                            self.ident[:])
                    c["ts"].inc(mm)
                    self.ev[("Tc", b, ii)] = c["ts"].n

            if i < SC:
                @self.blk.vector
                def _(v):
                    self.wte(v, "vector", c["ts"], ("PRJ", b, i))
                    self.wte(v, "vector", c["as_"], ("sqC", b, i - 2))
                    c["vs"].inc(v.reduce_sum(
                        self.col(self.tA, i % 2), psC[:], axis=AX))
                    self.wt(v, "vector", c["vs"], c["vs"].n)
                    c["vs"].inc(v.tensor_scalar_mul(
                        self.col(self.negm, i % 2),
                        self.col(self.tA, i % 2), -1.0 / E))
                self.ev[("negmC", b, i)] = c["vs"].n

                @self.blk.scalar
                def _(s):
                    self.wte(s, "scalar", c["vs"], ("negmC", b, i))
                    self.wte(s, "scalar", c["ts"], ("Tc", b, i - 2))
                    c["as_"].inc(s.activation(
                        out=ln2s[i % 2][:], in_=psC[:], func=AF.Square,
                        bias=self.col(self.negm, i % 2),
                        accum_out=self.col(self.ssqa, i % 2)))
                    self.ev[("sqC", b, i)] = c["as_"].n
                    self.wt(s, "scalar", c["as_"], c["as_"].n)
                    c["as_"].inc(s.activation(
                        out=self.col(self.t1c, i % 2),
                        in_=self.col(self.ssqa, i % 2), func=AF.Ln,
                        scale=1.0 / E, bias=self.col(self.cbias, 0)))
                    self.wt(s, "scalar", c["as_"], c["as_"].n)
                    c["as_"].inc(s.activation(
                        out=self.col(self.rstd, i % 2),
                        in_=self.col(self.t1c, i % 2), func=AF.Exp,
                        scale=-0.5))
                    self.ev[("rstdC", b, i)] = c["as_"].n

                @self.blk.vector
                def _(v):
                    self.wte(v, "vector", c["as_"], ("rstdC", b, i))
                    self.wte(v, "vector", c["syo"], ("yst", b, i - 1))
                    c["vs"].inc(v.tensor_scalar(
                        ystage[:], psC[:],
                        self.col(self.negm, i % 2),
                        self.col(self.rstd, i % 2),
                        op0=ALU.add, op1=ALU.mult))
                self.ev[("ap1", b, i)] = c["vs"].n

                @self.blk.gpsimd
                def _(g):
                    self.wte(g, "gpsimd", c["vs"], ("ap1", b, i))
                    c["sxc"].inc(g.dma_start(
                        out=ystage[:], in_=self.x[b, i * P:(i + 1) * P, :],
                        accum_op=ALU.add), 16)
                self.ev[("xacc", b, i)] = c["sxc"].n

                @self.blk.vector
                def _(v):
                    self.wte(v, "vector", c["sxc"], ("xacc", b, i))
                    c["vs"].inc(v.reduce_sum(
                        self.col(self.tB, i % 2), ystage[:], axis=AX))
                    self.wt(v, "vector", c["vs"], c["vs"].n)
                    c["vs"].inc(v.tensor_scalar_mul(
                        self.col(self.negm2, i % 2),
                        self.col(self.tB, i % 2), -1.0 / E))
                self.ev[("negm2", b, i)] = c["vs"].n

                @self.blk.gpsimd
                def _(g):
                    self.wte(g, "gpsimd", c["sxc"], ("xacc", b, i))
                    dst = bass.AP(self.ys.tensor, i * BLK, [[E, P], [1, E]])
                    c["syo"].inc(g.dma_start(out=dst, in_=ystage[:]), 16)
                self.ev[("yst", b, i)] = c["syo"].n

                @self.blk.scalar
                def _(s):
                    self.wte(s, "scalar", c["vs"], ("negm2", b, i))
                    c["as_"].inc(s.activation(
                        out=ln2s[i % 2][:], in_=ystage[:], func=AF.Square,
                        bias=self.col(self.negm2, i % 2),
                        accum_out=self.col(self.ssqb, i % 2)))
                    self.wt(s, "scalar", c["as_"], c["as_"].n)
                    c["as_"].inc(s.activation(
                        out=self.col(self.t1c, i % 2),
                        in_=self.col(self.ssqb, i % 2), func=AF.Ln,
                        scale=1.0 / E, bias=self.col(self.cbias, 0)))
                    self.wt(s, "scalar", c["as_"], c["as_"].n)
                    c["as_"].inc(s.activation(
                        out=self.col(self.rstd2, i % 2),
                        in_=self.col(self.t1c, i % 2), func=AF.Exp,
                        scale=-0.5))
                    self.ev[("rstd2", b, i)] = c["as_"].n

                @self.blk.vector
                def _(v):
                    self.wte(v, "vector", c["as_"], ("rstd2", b, i))
                    c["vs"].inc(v.tensor_scalar(
                        ln2s[i % 2][:], ystage[:],
                        self.col(self.negm2, i % 2),
                        self.col(self.rstd2, i % 2),
                        op0=ALU.add, op1=ALU.mult))
                self.ev[("ln2s", b, i)] = c["vs"].n

            if i - 1 >= 0:
                ii = i - 1

                @self.blk.scalar
                def _(s):
                    self.wte(s, "scalar", c["ts"], ("Tc", b, ii))
                    self.wte(s, "scalar", c["sl2o"], ("l2st", b, ii - 1))
                    c["as_"].inc(s.copy(l2stage[:], psTc[:]))
                self.ev[("evTc", b, ii)] = c["as_"].n

                @self.blk.gpsimd
                def _(g):
                    self.wte(g, "gpsimd", c["as_"], ("evTc", b, ii))
                    dst = bass.AP(self.ln2d.tensor, ii * P,
                                  [[E, P], [BLK, EC], [1, P]])
                    c["sl2o"].inc(g.dma_start(
                        out=dst,
                        in_=l2stage[:].rearrange("p (t r) -> p t r", r=P)),
                        16)
                self.ev[("l2st", b, ii)] = c["sl2o"].n
        sc.close()

    def phase_d(self, b):
        c, nc = self.c, self.nc
        sc = contextlib.ExitStack()
        ssb = lambda n, s, dt: sc.enter_context(nc.sbuf_tensor(n, s, dt))
        ln2fg = [ssb(f"d{b}_l2g{k}", [P, EC * GTOK], BF16) for k in range(2)]
        wtile = [ssb(f"d{b}_wt{k}", [P, E], BF16) for k in range(2)]
        w2sl = [ssb(f"d{b}_w2{k}", [P, 1024], BF16) for k in range(2)]
        projsb = [ssb(f"d{b}_pj{k}", [P, E], BF16) for k in range(GSTR)]
        rstage = [ssb(f"d{b}_rs{k}", [P, NB], F32) for k in range(2)]
        g2st = [ssb(f"d{b}_g2{k}", [P, NB], BF16) for k in range(2)]
        outstage = ssb(f"d{b}_outst", [P, E], F32)

        def l2g_load(g_):
            @self.blk.sync
            def _(sy):
                if g_ == 0:
                    self.ldbar(sy, "sync")
                self.wte(sy, "sync", c["sl2o"], ("l2st", b, SC - 1))
                self.wte(sy, "sync", c["ts"], ("fc", b, g_ - 2, HC - 1))
                for tt in range(EC):
                    src = bass.AP(self.ln2d.tensor, tt * BLK + g_ * GTOK,
                                  [[E, P], [1, GTOK]])
                    c["sl2i"].inc(sy.dma_start(
                        out=ln2fg[g_ % 2][:, tt * GTOK:(tt + 1) * GTOK],
                        in_=src), 16)
            self.ev[("l2ld", b, g_)] = c["sl2i"].n

        def w1_load(g_, jc):
            @self.blk.sync
            def _(sy):
                if g_ == 0 and jc < 2:
                    self.ldbar(sy, "sync")
                self.wte(sy, "sync", c["stro"], ("w1T_done",))
                if jc >= 2:
                    self.wte(sy, "sync", c["ts"], ("fc", b, g_, jc - 2))
                else:
                    self.wte(sy, "sync", c["ts"],
                             ("fc", b, g_ - 1, HC + jc - 2))
                src = bass.AP(self.w1qT_d.tensor, jc * BLK, [[E, P], [1, E]])
                sw = c["sw1a"] if jc % 2 == 0 else c["sw1b"]
                sw.inc(sy.dma_start(out=wtile[jc % 2][:], in_=src), 16)
            sw = c["sw1a"] if jc % 2 == 0 else c["sw1b"]
            self.ev[("w1ld", b, g_, jc)] = sw.n

        l2g_load(0)
        for g_ in range(NGRP):
            scF = contextlib.ExitStack()
            psF = [scF.enter_context(
                nc.psum_tensor(f"d{b}_{g_}_psF{k}", [P, NB], F32))
                for k in range(2)]
            psR = scF.enter_context(
                nc.psum_tensor(f"d{b}_{g_}_psR", [P, NB], F32))

            w1_load(g_, 0)
            w1_load(g_, 1)
            for jc in range(HC):
                @self.blk.tensor
                def _(t):
                    sw = c["sw1a"] if jc % 2 == 0 else c["sw1b"]
                    self.wte(t, "tensor", sw, ("w1ld", b, g_, jc))
                    self.wte(t, "tensor", c["sl2i"], ("l2ld", b, g_))
                    self.wte(t, "tensor", c["as_"], ("relu", b, g_, jc - 2))
                    if g_ == 0 and jc == 0:
                        self.wte(t, "tensor", c["vs"], ("ln2s", b, SC - 1))
                    if jc == 0:
                        self.wte(t, "tensor", c["ts"],
                                 ("proj", b, g_ - 1, 1, HC - 1))
                        for st in range(GSTR):
                            self.wte(t, "tensor", c["vs"],
                                     ("evP", b, g_ - 1, 1, st))
                        if g_ == 0:
                            self.wte(t, "tensor", c["ts"], ("PRJ", b, SC - 1))
                    for dc in range(EC):
                        mm = t.matmul(
                            psF[jc % 2][:, 0:NB],
                            wtile[jc % 2][:, dc * P:(dc + 1) * P],
                            ln2fg[g_ % 2][:, dc * GTOK:dc * GTOK + NB],
                            start=(dc == 0), stop=(dc == EC - 1))
                    c["ts"].inc(mm)
                    self.ev[("fc", b, g_, jc)] = c["ts"].n
                    if jc - 2 >= 0:
                        self.wte(t, "tensor", c["as_"],
                                 ("g2", b, g_, jc - 2))
                        self.wte(t, "tensor", c["scst"], ("consts",))
                        mm = t.matmul(
                            psR[0:1, 0:NB], self.ones[:],
                            g2st[(jc - 2) % 2][:],
                            start=(jc - 2 == 0), stop=(jc - 2 == HC - 1))
                        c["ts"].inc(mm)
                        self.ev[("strow", b, g_, jc - 2)] = c["ts"].n

                @self.blk.scalar
                def _(s):
                    self.wte(s, "scalar", c["ts"], ("fc", b, g_, jc))
                    self.wte(s, "scalar", c["vs"], ("gg", b, g_, jc - 2))
                    c["as_"].inc(s.activation(
                        out=rstage[jc % 2][:], in_=psF[jc % 2][:, 0:NB],
                        func=AF.Relu))
                self.ev[("relu", b, g_, jc)] = c["as_"].n

                @self.blk.vector
                def _(v):
                    self.wte(v, "vector", c["as_"], ("relu", b, g_, jc))
                    c["vs"].inc(v.tensor_mul(
                        self.big0[:, jc * NB:(jc + 1) * NB],
                        rstage[jc % 2][:], rstage[jc % 2][:]))
                self.ev[("gg", b, g_, jc)] = c["vs"].n

                @self.blk.scalar
                def _(s):
                    self.wte(s, "scalar", c["vs"], ("gg", b, g_, jc))
                    self.wte(s, "scalar", c["ts"], ("strow", b, g_, jc - 2))
                    c["as_"].inc(s.activation(
                        out=g2st[jc % 2][:],
                        in_=self.big0[:, jc * NB:(jc + 1) * NB],
                        func=AF.Square))
                self.ev[("g2", b, g_, jc)] = c["as_"].n

                if jc + 2 < HC:
                    w1_load(g_, jc + 2)
                elif jc == HC - 1 and g_ + 1 < NGRP:
                    l2g_load(g_ + 1)

            @self.blk.tensor
            def _(t):
                for jt in (HC - 2, HC - 1):
                    self.wte(t, "tensor", c["as_"], ("g2", b, g_, jt))
                    mm = t.matmul(
                        psR[0:1, 0:NB], self.ones[:], g2st[jt % 2][:],
                        start=(jt == 0), stop=(jt == HC - 1))
                    c["ts"].inc(mm)
                    self.ev[("strow", b, g_, jt)] = c["ts"].n

            @self.blk.vector
            def _(v):
                self.wte(v, "vector", c["ts"], ("strow", b, g_, HC - 1))
                c["vs"].inc(v.tensor_copy(self.ssqrow[0:1, :], psR[0:1, :]))
            self.ev[("ssqrow", b, g_)] = c["vs"].n

            @self.blk.scalar
            def _(s):
                self.wte(s, "scalar", c["vs"], ("ssqrow", b, g_))
                c["as_"].inc(s.activation(
                    out=self.lamrow[0:1, :], in_=self.ssqrow[0:1, :],
                    func=AF.Ln, scale=1.0 / H, bias=self.cbias[0:1, 1:2]))
                self.wt(s, "scalar", c["as_"], c["as_"].n)
                c["as_"].inc(s.activation(
                    out=self.lamrow[0:1, :], in_=self.lamrow[0:1, :],
                    func=AF.Exp, scale=-0.5))
            self.ev[("lrow", b, g_)] = c["as_"].n

            @self.blk.gpsimd
            def _(g):
                self.wte(g, "gpsimd", c["as_"], ("lrow", b, g_))
                dst = bass.AP(self.rowd.tensor, 0, [[1, 1], [1, GTOK]])
                c["srb"].inc(g.dma_start(out=dst, in_=self.lamrow[0:1, :]),
                             16)
            self.ev[("rbo", b, g_)] = c["srb"].n

            @self.blk.sync
            def _(sy):
                self.wte(sy, "sync", c["srb"], ("rbo", b, g_))
                src = bass.AP(self.rowd.tensor, 0, [[1, P], [P, GSTR]])
                c["srb"].inc(sy.dma_start(
                    out=self.lamcol[:], in_=src,
                    allow_slow_non_contiguous=True), 16)
            self.ev[("lam", b, g_)] = c["srb"].n

            if self.phases == "Dfc":
                scF.close()
                break
            scF.close()
            scP = contextlib.ExitStack()
            psP = [scP.enter_context(
                nc.psum_tensor(f"d{b}_{g_}_psP{k}", [P, 1024], F32))
                for k in range(GSTR)]

            def w2_load(dbh, jc):
                @self.blk.sync
                def _(sy):
                    if g_ == 0 and dbh == 0 and jc < 2:
                        self.ldbar(sy, "sync")
                    self.wte(sy, "sync", c["stro"], ("w2T_done",))
                    if jc >= 2:
                        self.wte(sy, "sync", c["ts"],
                                 ("proj", b, g_, dbh, jc - 2))
                    elif dbh == 1:
                        self.wte(sy, "sync", c["ts"],
                                 ("proj", b, g_, 0, HC + jc - 2))
                    else:
                        self.wte(sy, "sync", c["ts"],
                                 ("proj", b, g_ - 1, 1, HC + jc - 2))
                    src = bass.AP(self.w2T_d.tensor, jc * BLK + dbh * 1024,
                                  [[E, P], [1, 1024]])
                    sw = c["sw2a"] if jc % 2 == 0 else c["sw2b"]
                    sw.inc(sy.dma_start(out=w2sl[jc % 2][:], in_=src), 16)
                sw = c["sw2a"] if jc % 2 == 0 else c["sw2b"]
                self.ev[("w2ld", b, g_, dbh, jc)] = sw.n

            for dbh in range(2):
                w2_load(dbh, 0)
                w2_load(dbh, 1)
                for jc in range(HC):
                    @self.blk.tensor
                    def _(t):
                        sw = c["sw2a"] if jc % 2 == 0 else c["sw2b"]
                        self.wte(t, "tensor", sw, ("w2ld", b, g_, dbh, jc))
                        self.wte(t, "tensor", c["as_"],
                                 ("g2", b, g_, HC - 1))
                        if jc == 0:
                            if dbh == 0:
                                self.wte(t, "tensor", c["as_"],
                                         ("relu", b, g_, HC - 1))
                                self.wte(t, "tensor", c["vs"],
                                         ("ssqrow", b, g_))
                            else:
                                for st in range(GSTR):
                                    self.wte(t, "tensor", c["vs"],
                                             ("evP", b, g_, 0, st))
                        for st in range(GSTR):
                            for d2 in range(2):
                                mm = t.matmul(
                                    psP[st][:, d2 * NB:(d2 + 1) * NB],
                                    self.big0[:, jc * NB + st * P:
                                              jc * NB + st * P + P],
                                    w2sl[jc % 2][:, d2 * NB:(d2 + 1) * NB],
                                    start=(jc == 0), stop=(jc == HC - 1))
                        c["ts"].inc(mm)
                        self.ev[("proj", b, g_, dbh, jc)] = c["ts"].n
                    if jc + 2 < HC:
                        w2_load(dbh, jc + 2)

                for st in range(GSTR):
                    @self.blk.vector
                    def _(v):
                        self.wte(v, "vector", c["ts"],
                                 ("proj", b, g_, dbh, HC - 1))
                        if dbh == 0:
                            self.wte(v, "vector", c["sout"],
                                     ("outst", b, g_ - 1, st))
                        c["vs"].inc(v.tensor_copy(
                            projsb[st][:, dbh * 1024:(dbh + 1) * 1024],
                            psP[st][:]))
                    self.ev[("evP", b, g_, dbh, st)] = c["vs"].n

            for st in range(GSTR):
                gi = g_ * GSTR + st

                @self.blk.vector
                def _(v):
                    self.wte(v, "vector", c["vs"], ("evP", b, g_, 1, st))
                    self.wte(v, "vector", c["as_"], ("sqD", b, g_, st - 2))
                    c["vs"].inc(v.reduce_sum(
                        self.col(self.tA, st % 2), projsb[st][:], axis=AX))
                    self.wt(v, "vector", c["vs"], c["vs"].n)
                    c["vs"].inc(v.tensor_scalar_mul(
                        self.col(self.negm3, st % 2),
                        self.col(self.tA, st % 2), -1.0 / E))
                self.ev[("negm3", b, g_, st)] = c["vs"].n

                @self.blk.scalar
                def _(s):
                    self.wte(s, "scalar", c["vs"], ("negm3", b, g_, st))
                    key = (("outst", b, g_, st - 1) if st > 0 else
                           ("outst", b, g_ - 1, GSTR - 1))
                    self.wte(s, "scalar", c["sout"], key)
                    c["as_"].inc(s.activation(
                        out=outstage[:], in_=projsb[st][:], func=AF.Square,
                        bias=self.col(self.negm3, st % 2),
                        accum_out=self.col(self.ssqa, st % 2)))
                self.ev[("sqD", b, g_, st)] = c["as_"].n

                @self.blk.vector
                def _(v):
                    self.wte(v, "vector", c["as_"], ("sqD", b, g_, st))
                    self.wte(v, "vector", c["srb"], ("lam", b, g_))
                    c["vs"].inc(v.tensor_scalar_mul(
                        self.col(self.tA, st % 2),
                        self.col(self.ssqa, st % 2), 1.0 / E))
                    self.wt(v, "vector", c["vs"], c["vs"].n)
                    c["vs"].inc(v.tensor_mul(
                        self.col(self.tB, st % 2),
                        self.lamcol[:, st:st + 1],
                        self.lamcol[:, st:st + 1]))
                    self.wt(v, "vector", c["vs"], c["vs"].n)
                    c["vs"].inc(v.tensor_mul(
                        self.col(self.tA, st % 2),
                        self.col(self.tA, st % 2),
                        self.col(self.tB, st % 2)))
                self.ev[("varD", b, g_, st)] = c["vs"].n

                @self.blk.scalar
                def _(s):
                    self.wte(s, "scalar", c["vs"], ("varD", b, g_, st))
                    c["as_"].inc(s.activation(
                        out=self.col(self.t1c, st % 2),
                        in_=self.col(self.tA, st % 2), func=AF.Ln,
                        bias=self.col(self.cbias, 0)))
                    self.wt(s, "scalar", c["as_"], c["as_"].n)
                    c["as_"].inc(s.activation(
                        out=self.col(self.rstd, st % 2),
                        in_=self.col(self.t1c, st % 2), func=AF.Exp,
                        scale=-0.5))
                self.ev[("rsD", b, g_, st)] = c["as_"].n

                @self.blk.vector
                def _(v):
                    self.wte(v, "vector", c["as_"], ("rsD", b, g_, st))
                    c["vs"].inc(v.tensor_mul(
                        self.col(self.lam, st % 2),
                        self.col(self.rstd, st % 2),
                        self.lamcol[:, st:st + 1]))
                    self.wt(v, "vector", c["vs"], c["vs"].n)
                    c["vs"].inc(v.tensor_scalar(
                        outstage[:], projsb[st][:],
                        self.col(self.negm3, st % 2),
                        self.col(self.lam, st % 2),
                        op0=ALU.add, op1=ALU.mult))
                self.ev[("outpre", b, g_, st)] = c["vs"].n

                @self.blk.gpsimd
                def _(g):
                    self.wte(g, "gpsimd", c["vs"], ("outpre", b, g_, st))
                    src = bass.AP(self.ys.tensor, gi * BLK, [[E, P], [1, E]])
                    c["sya"].inc(g.dma_start(
                        out=outstage[:], in_=src, accum_op=ALU.add), 16)
                    self.ev[("yacc", b, g_, st)] = c["sya"].n
                    self.wt(g, "gpsimd", c["sya"], c["sya"].n)
                    c["sout"].inc(g.dma_start(
                        out=self.out[b, gi * P:(gi + 1) * P, :],
                        in_=outstage[:]), 16)
                self.ev[("outst", b, g_, st)] = c["sout"].n

            scP.close()
            if self.phases == "Dg0":
                break
            if self.phases == "Dg1" and g_ == 1:
                break
            if self.phases == "Dg2" and g_ == 2:
                break
            if g_ == NGRP - 1 and b + 1 < NBATCH:
                @self.blk.gpsimd
                def _(g):
                    self.wte(g, "gpsimd", c["ts"], ("PRJ", b, SC - 1))
                    for cch in range(EC):
                        c["swq"].inc(g.dma_start(
                            out=self.big2[:, cch * E:(cch + 1) * E],
                            in_=self.wq[cch * P:(cch + 1) * P, :]), 16)
                self.ev[("wqld", b + 1)] = c["swq"].n
        self.ev[("outdone", b)] = c["sout"].n
        sc.close()

    def finale(self):
        c = self.c

        @self.blk.sync
        def _(sy):
            for n in ("sout", "syo", "sqo0", "sqo1", "svo0", "svo1",
                      "sqs0", "sqs1", "stro", "sl2o", "sya", "sxc", "srb"):
                self.wt(sy, "sync", c[n], c[n].n)

        if self.debug:
            @self.blk.sync
            def _(sy):
                for n, v in self.barrier_vals().items():
                    self.wt(sy, "sync", c[n], v)
                sy.dma_start(out=self.dbg0, in_=self.big0[:]).then_inc(
                    c["sout"].h, 16)
                sy.dma_start(out=self.dbg2, in_=self.big2[:]).then_inc(
                    c["sout"].h, 16)
                sy.wait_ge(c["sout"].h, c["sout"].n + 32)


_NC_CACHE = {}


def _get_nc(debug=None, phases=None):
    key = (debug, phases)
    if key not in _NC_CACHE:
        _NC_CACHE[key] = MK(debug=debug, phases=phases).build()
    return _NC_CACHE[key]


def _in_maps(x, wq, wk, wv, wo, wfc, wproj, ncores=8):
    x = np.ascontiguousarray(np.asarray(x, dtype=np.float32))
    ident = np.eye(P, dtype=ml_dtypes.bfloat16)
    ones = np.ones((P, 1), dtype=ml_dtypes.bfloat16)
    cbias = np.tile(np.array([[EPS_LN, EPS_RMS]], dtype=np.float32), (P, 1))
    mk = lambda a: np.ascontiguousarray(np.asarray(a, dtype=np.float32))
    maps = []
    for cidx in range(ncores):
        maps.append({
            "x": x[cidx * NBATCH:(cidx + 1) * NBATCH],
            "wq": mk(wq), "wk": mk(wk), "wv": mk(wv), "wo": mk(wo),
            "wfc": mk(wfc), "wproj": mk(wproj),
            "ident_in": ident, "ones_in": ones, "cbias_in": cbias,
        })
    return maps


def kernel(L0_attn_x, L0_attn_Wq, L0_attn_Wk, L0_attn_Wv, L0_attn_Wo,
           L0_ffn_W_fc, L0_ffn_W_proj):
    nc = _get_nc()
    maps = _in_maps(L0_attn_x, L0_attn_Wq, L0_attn_Wk, L0_attn_Wv,
                    L0_attn_Wo, L0_ffn_W_fc, L0_ffn_W_proj)
    res = run_bass_kernel_spmd(nc, maps, core_ids=list(range(8)))
    out = np.concatenate([np.asarray(r["out"]) for r in res.results], axis=0)
    return out.astype(np.float32)


# version 03
# speedup vs baseline: 1.0029x; 1.0029x over previous
import contextlib
import numpy as np
import ml_dtypes

import concourse.bass as bass
import concourse.mybir as mybir
from concourse.bass_utils import run_bass_kernel_spmd

F32 = mybir.dt.float32
BF16 = mybir.dt.bfloat16
AF = mybir.ActivationFunctionType
ALU = mybir.AluOpType
AX = mybir.AxisListType.X

P = 128
S = 2048
E = 2048
H = 8192
NB = 512
SC = S // P
EC = E // P
HC = H // P
NBATCH = 2
GTOK = 512
NGRP = S // GTOK
GSTR = GTOK // P
BLK = P * E

EPS_LN = 1e-5
EPS_RMS = float(np.finfo(np.float32).eps)
NAT_LOG_EXP_SET = 6


class Ctr:
    def __init__(self, es, nc, name):
        self.h = es.enter_context(nc.semaphore(name))
        self.name = name
        self.n = 0

    def inc(self, instr, k=1):
        instr.then_inc(self.h, k)
        self.n += k
        return self.n


class MK:
    def __init__(self, debug=None, phases=None):
        self.debug = debug
        self.phases = phases
        nc = bass.Bass("TRN2", target_bir_lowering=False)
        self.nc = nc
        es = contextlib.ExitStack()
        self.es = es
        nc._mk_es = es

        d = nc.dram_tensor
        self.x = d("x", [NBATCH, S, E], F32, kind="ExternalInput").ap()
        self.wq = d("wq", [E, E], F32, kind="ExternalInput").ap()
        self.wk = d("wk", [E, E], F32, kind="ExternalInput").ap()
        self.wv = d("wv", [E, E], F32, kind="ExternalInput").ap()
        self.wo = d("wo", [E, E], F32, kind="ExternalInput").ap()
        self.wfc = d("wfc", [H, E], F32, kind="ExternalInput").ap()
        self.wproj = d("wproj", [E, H], F32, kind="ExternalInput").ap()
        self.ident_in = d("ident_in", [P, P], BF16, kind="ExternalInput").ap()
        self.ones_in = d("ones_in", [P, 1], BF16, kind="ExternalInput").ap()
        self.cbias_in = d("cbias_in", [P, 2], F32, kind="ExternalInput").ap()
        self.out = d("out", [NBATCH, S, E], F32, kind="ExternalOutput").ap()

        dk = "ExternalOutput" if debug else "Internal"
        self.qsl = d("qsl", [EC * BLK], BF16, kind=dk).ap()
        self.vfs = d("vfs", [EC * BLK], BF16, kind=dk).ap()
        self.ys = d("ys", [S * E], F32, kind=dk).ap()
        self.ln2d = d("ln2d", [EC * BLK], BF16, kind=dk).ap()
        self.w1q_d = d("w1q_d", [H * E], BF16, kind=dk).ap()
        self.w1qT_d = d("w1qT_d", [HC * BLK], BF16, kind=dk).ap()
        self.w2q_d = d("w2q_d", [E * H], BF16, kind=dk).ap()
        self.w2T_d = d("w2T_d", [HC * BLK], BF16, kind=dk).ap()
        self.rowd = d("rowd", [GTOK], F32, kind="Internal").ap()
        if debug:
            self.dbg0 = d("dbg0", [P, EC * E], BF16, kind="ExternalOutput").ap()
            self.dbg2 = d("dbg2", [P, EC * E], BF16, kind="ExternalOutput").ap()
            self.dbg3 = d("dbg3", [P, E], BF16, kind="ExternalOutput").ap()
            self.dbg4 = d("dbg4", [P, E], BF16, kind="ExternalOutput").ap()

        def sb(name, shape, dt):
            return es.enter_context(nc.sbuf_tensor(name, shape, dt))

        self.big0 = sb("big0", [P, EC * E], BF16)
        self.big2 = sb("big2", [P, EC * E], BF16)
        self.ident = sb("ident", [P, P], BF16)
        self.ones = sb("ones", [P, 1], BF16)
        self.rowrec = sb("rowrec", [P, SC], F32)
        for nm in ("negm", "rstd", "ssqa", "ssqb", "t1c", "mxc", "nmx8",
                   "srow", "negm2", "rstd2", "negm3", "lam", "tA", "tB"):
            setattr(self, nm, sb(nm, [P, 2], F32))
        self.lamcol = sb("lamcol", [P, GSTR], F32)
        self.cbias = sb("cbias", [P, 2], F32)
        self.ssqrow = sb("ssqrow", [1, GTOK], F32)
        self.lamrow = self.ssqrow

        names = ["sx0", "sx1", "swq", "swk", "swv0", "swv1", "swo",
                 "sqo0", "sqo1", "sqi0", "sqi1", "svo0", "svo1", "svi",
                 "syo", "sl2o", "sl2i", "sw1a", "sw1b", "sw2a", "sw2b",
                 "sql0", "sql1", "sqs0", "sqs1", "stri", "stro", "srb",
                 "sya", "sout", "sxc", "scst", "ts", "vs", "as_", "ps",
                 "svia", "svib", "svic", "svid", "swoa", "swob", "swoc",
                 "swod"]
        self.c = {n: Ctr(es, nc, n) for n in names}
        self.ev = {}
        self.lastw = {}
        self.blk = es.enter_context(nc.Block())

        self.quant_cursor = 0
        self.quant_loads_issued = 0
        self.transp1_cursor = 0
        self.transp2_cursor = 0

    def wt(self, eng, ename, ctr, val):
        if val is None or val <= 0:
            return
        lw = self.lastw.setdefault(ename, {})
        if lw.get(ctr.name, -1) >= val:
            return
        eng.wait_ge(ctr.h, val)
        lw[ctr.name] = val

    def wte(self, eng, ename, ctr, key):
        if key in self.ev:
            self.wt(eng, ename, ctr, self.ev[key])

    def col(self, t, k):
        return t[:, k:k + 1]

    def barrier_vals(self):
        return {n: self.c[n].n for n in ("ts", "vs", "as_", "ps")}

    def ldbar(self, eng, ename):
        for n in ("ts", "vs", "as_", "ps", "sqo0", "sqo1", "svo0", "svo1",
                  "syo", "sl2o", "sqs0", "sqs1", "sout", "sya", "sxc",
                  "stri", "stro"):
            self.wt(eng, ename, self.c[n], self.c[n].n)

    def build(self):
        self.preamble()
        stop = self.phases
        for b in range(NBATCH):
            bs = f"b{b}"
            self.phase_a(b)
            if stop in ("A", bs + "A"):
                break
            self.phase_a1b(b)
            if stop in ("K", bs + "K"):
                break
            self.phase_a2(b)
            if stop in ("V", bs + "V"):
                break
            self.phase_b1(b)
            if stop in ("B1", bs + "B1"):
                break
            self.phase_b2(b)
            if stop in ("B2", bs + "B2"):
                break
            self.phase_c(b)
            if stop in ("C", bs + "C"):
                break
            self.phase_d(b)
            if stop in ("D", "Dfc", "Dg0", "Dg1", "Dg2", bs + "D"):
                break
        self.finale()
        return self.nc

    def preamble(self):
        c = self.c

        @self.blk.scalar
        def _(s):
            s.add_instruction(
                mybir.InstLoadActFuncSet(
                    name=self.nc.get_next_instruction_name(),
                    act_func_set_id=NAT_LOG_EXP_SET,
                    ins=[], outs=[],
                )
            )

        @self.blk.sync
        def _(sy):
            c["scst"].inc(sy.dma_start(out=self.ident[:], in_=self.ident_in), 16)
            c["scst"].inc(sy.dma_start(out=self.ones[:], in_=self.ones_in), 16)
            c["scst"].inc(sy.dma_start(out=self.cbias[:], in_=self.cbias_in), 16)
        self.ev[("consts",)] = c["scst"].n

        @self.blk.gpsimd
        def _(g):
            for cch in range(EC):
                ins = g.dma_start(
                    out=self.big2[:, cch * E:(cch + 1) * E],
                    in_=self.wq[cch * P:(cch + 1) * P, :])
                c["swq"].inc(ins, 16)
        self.ev[("wqld", 0)] = c["swq"].n

        self.qscope = contextlib.ExitStack()
        qsb = lambda n, s, dt: self.qscope.enter_context(
            self.nc.sbuf_tensor(n, s, dt))
        self.q_wsl = [qsb(f"q_wsl{k}", [P, E], F32) for k in range(2)]
        self.q_wb = [qsb(f"q_wb{k}", [P, E], BF16) for k in range(2)]
        self.q_quot = [qsb(f"q_quot{k}", [P, E], BF16) for k in range(2)]
        self.q_sgn = [qsb(f"q_sgn{k}", [P, E], BF16) for k in range(2)]
        self.q_sums = [qsb(f"q_sums{k}", [P, E // 64], F32) for k in range(2)]
        self.q_scl = [qsb(f"q_scl{k}", [P, E // 64], BF16) for k in range(2)]
        self.q_thr = [qsb(f"q_thr{k}", [P, E // 64], F32) for k in range(2)]
        self.trscope = contextlib.ExitStack()
        self.trbuf = self.trscope.enter_context(
            self.nc.sbuf_tensor("trbuf", [P, E], BF16))

    def quant_unit_src(self, u):
        if u < 64:
            src = bass.AP(self.wfc.tensor, u * P * E, [[E, P], [1, E]])
            dst = bass.AP(self.w1q_d.tensor, u * P * E, [[E, P], [1, E]])
        else:
            v = u - 64
            dc, qq = v // 4, v % 4
            off = dc * P * H + qq * E
            src = bass.AP(self.wproj.tensor, off, [[H, P], [1, E]])
            dst = bass.AP(self.w2q_d.tensor, off, [[H, P], [1, E]])
        return src, dst

    def quant_issue_load(self):
        u = self.quant_loads_issued
        if u >= 128:
            return
        self.quant_loads_issued += 1
        c = self.c
        src, _ = self.quant_unit_src(u)

        @self.blk.scalar
        def _(s):
            self.wte(s, "scalar", c["as_"], ("qcast", u - 2))
            c[f"sql{u % 2}"].inc(
                s.dma_start(out=self.q_wsl[u % 2][:], in_=src), 16)
        self.ev[("qld", u)] = c[f"sql{u % 2}"].n

    def quant_step(self, n):
        for _ in range(n):
            u = self.quant_cursor
            if u >= 128:
                return
            self.quant_cursor += 1
            while self.quant_loads_issued < min(u + 2, 128):
                self.quant_issue_load()
            c = self.c
            k = u % 2
            wsl = self.q_wsl[k]
            wb = self.q_wb[k]
            absq = self.q_quot[k]
            sgn = self.q_sgn[k]
            sums, scl, thr = self.q_sums[k], self.q_scl[k], self.q_thr[k]

            @self.blk.scalar
            def _(s):
                self.wte(s, "scalar", c[f"sql{k}"], ("qld", u))
                self.wte(s, "scalar", c["vs"], ("qw", u - 2))
                self.wte(s, "scalar", c[f"sqs{k}"], ("qst", u - 2))
                c["as_"].inc(s.copy(wb[:], wsl[:]))
                self.ev[("qcast", u)] = c["as_"].n
                self.wt(s, "scalar", c["as_"], c["as_"].n)
                c["as_"].inc(s.activation(out=absq[:], in_=wb[:], func=AF.Abs))
                self.wt(s, "scalar", c["as_"], c["as_"].n)
                c["as_"].inc(s.activation(out=sgn[:], in_=wb[:], func=AF.Sign))
            self.ev[("qact", u)] = c["as_"].n

            @self.blk.vector
            def _(v):
                sv = lambda: self.wt(v, "vector", c["vs"], c["vs"].n)
                self.wte(v, "vector", c["as_"], ("qcast", u))
                c["vs"].inc(v.tensor_reduce(
                    sums[:], wb[:].rearrange("p (g k) -> p g k", k=64),
                    axis=AX, op=ALU.add, apply_absolute_value=True))
                sv()
                c["vs"].inc(v.tensor_scalar(
                    scl[:], sums[:], 1.0 / 64, 1e-8,
                    op0=ALU.mult, op1=ALU.max))
                sv()
                c["vs"].inc(v.tensor_scalar_mul(thr[:], scl[:], 0.5))
                sv()
                self.wte(v, "vector", c["as_"], ("qact", u))
                thr_bc = bass.AP(thr, 0, [[E // 64, P], [1, E // 64], [0, 64]])
                c["vs"].inc(v.tensor_tensor(
                    absq[:].rearrange("p (g k) -> p g k", k=64),
                    absq[:].rearrange("p (g k) -> p g k", k=64),
                    thr_bc, op=ALU.is_gt))
                sv()
                scl_bc = bass.AP(scl, 0, [[E // 64, P], [1, E // 64], [0, 64]])
                c["vs"].inc(v.tensor_tensor(
                    sgn[:].rearrange("p (g k) -> p g k", k=64),
                    sgn[:].rearrange("p (g k) -> p g k", k=64),
                    scl_bc, op=ALU.mult))
                sv()
                c["vs"].inc(v.tensor_mul(wb[:], absq[:], sgn[:]))
            self.ev[("qw", u)] = c["vs"].n

            @self.blk.scalar
            def _(s):
                self.wte(s, "scalar", c["vs"], ("qw", u))
                _, dst = self.quant_unit_src(u)
                c[f"sqs{k}"].inc(s.dma_start(out=dst, in_=wb[:]), 16)
            self.ev[("qst", u)] = c[f"sqs{k}"].n
            while self.quant_loads_issued < min(u + 3, 128):
                self.quant_issue_load()
            if u == 63:
                self.ev[("qfc_done",)] = (c["sqs0"].n, c["sqs1"].n)
            if u == 127:
                self.ev[("qpr_done",)] = (c["sqs0"].n, c["sqs1"].n)

    def transp1_step(self, n):
        if ("qfc_done",) not in self.ev:
            return
        for _ in range(n):
            u = self.transp1_cursor
            if u >= EC * 4:
                return
            self.transp1_cursor += 1
            dc, q = u // 4, u % 4
            c = self.c

            @self.blk.sync
            def _(sy):
                v0, v1 = self.ev[("qfc_done",)]
                self.wt(sy, "sync", c["sqs0"], v0)
                self.wt(sy, "sync", c["sqs1"], v1)
                self.wte(sy, "sync", c["stro"], ("t1o", u - 1))
                src = bass.AP(self.w1q_d.tensor, q * 2048 * E + dc * P,
                              [[E, 2048], [1, P]])
                c["stri"].inc(
                    sy.dma_start(out=self.trbuf[:], in_=src, transpose=True),
                    16)
            self.ev[("t1i", u)] = c["stri"].n

            @self.blk.gpsimd
            def _(g):
                self.wte(g, "gpsimd", c["stri"], ("t1i", u))
                dst = bass.AP(
                    self.w1qT_d.tensor,
                    q * 16 * BLK + dc * P,
                    [[E, P], [BLK, 16], [1, P]])
                c["stro"].inc(g.dma_start(
                    out=dst,
                    in_=self.trbuf[:].rearrange("p (j i) -> p j i", i=P)), 16)
            self.ev[("t1o", u)] = c["stro"].n
            if u == EC * 4 - 1:
                self.ev[("w1T_done",)] = c["stro"].n

    def transp2_step(self, n):
        if ("qpr_done",) not in self.ev:
            return
        for _ in range(n):
            jc = self.transp2_cursor
            if jc >= HC:
                return
            self.transp2_cursor += 1
            c = self.c

            @self.blk.sync
            def _(sy):
                v0, v1 = self.ev[("qpr_done",)]
                self.wt(sy, "sync", c["sqs0"], v0)
                self.wt(sy, "sync", c["sqs1"], v1)
                self.wte(sy, "sync", c["stro"], ("t2o", jc - 1))
                if jc == 0:
                    self.wte(sy, "sync", c["stro"], ("w1T_done",))
                src = bass.AP(self.w2q_d.tensor, jc * P, [[H, E], [1, P]])
                c["stri"].inc(sy.dma_start(
                    out=self.trbuf[:, 0:E], in_=src, transpose=True), 16)
            self.ev[("t2i", jc)] = c["stri"].n

            @self.blk.gpsimd
            def _(g):
                self.wte(g, "gpsimd", c["stri"], ("t2i", jc))
                dst = bass.AP(self.w2T_d.tensor, jc * BLK, [[E, P], [1, E]])
                c["stro"].inc(g.dma_start(out=dst, in_=self.trbuf[:, 0:E]),
                              16)
            self.ev[("t2o", jc)] = c["stro"].n
            if jc == HC - 1:
                self.ev[("w2T_done",)] = c["stro"].n

    def phase_a(self, b):
        c, nc = self.c, self.nc
        sc = contextlib.ExitStack()
        ssb = lambda n, s, dt: sc.enter_context(nc.sbuf_tensor(n, s, dt))
        sps = lambda n, s, dt: sc.enter_context(nc.psum_tensor(n, s, dt))
        xst = [ssb(f"a{b}_xst{k}", [P, E], F32) for k in range(2)]
        ln1s = [ssb(f"a{b}_ln1s{k}", [P, E], BF16) for k in range(2)]
        qstage = [ssb(f"a{b}_qst{k}", [P, E], BF16) for k in range(2)]
        psQ = sps(f"a{b}_psQ", [P, E], F32)
        psT = sps(f"a{b}_psT", [P, E], BF16)
        bar = self.barrier_vals()

        @self.blk.sync
        def _(sy):
            for n, v in bar.items():
                self.wt(sy, "sync", c[n], v)
            self.wte(sy, "sync", c["sout"], ("outdone", b - 1))
            for i in (0, 1):
                ins = sy.dma_start(
                    out=xst[i % 2][:], in_=self.x[b, i * P:(i + 1) * P, :])
                c[f"sx{i % 2}"].inc(ins, 16)
                self.ev[("xld", b, i)] = c[f"sx{i % 2}"].n

        for i in range(SC + 1):
            if i < SC:
                @self.blk.vector
                def _(v):
                    self.wte(v, "vector", c[f"sx{i % 2}"], ("xld", b, i))
                    self.wte(v, "vector", c["as_"], ("sq", b, i - 2))
                    c["vs"].inc(v.reduce_sum(
                        self.col(self.tA, i % 2), xst[i % 2][:], axis=AX))
                    self.wt(v, "vector", c["vs"], c["vs"].n)
                    c["vs"].inc(v.tensor_scalar_mul(
                        self.col(self.negm, i % 2),
                        self.col(self.tA, i % 2), -1.0 / E))
                self.ev[("negm", b, i)] = c["vs"].n

            @self.blk.scalar
            def _(s):
                if i - 1 >= 0:
                    self.wte(s, "scalar", c["ts"], ("T", b, i - 1))
                    dst = bass.AP(self.big0, (i - 1) * P,
                                  [[EC * E, P], [E, EC], [1, P]])
                    c["as_"].inc(s.copy(
                        dst, psT[:].rearrange("p (t r) -> p t r", r=P)))
                    self.ev[("evT", b, i - 1)] = c["as_"].n
                if i < SC:
                    self.wte(s, "scalar", c["scst"], ("consts",))
                    self.wte(s, "scalar", c["vs"], ("negm", b, i))
                    self.wte(s, "scalar", c["ts"], ("T", b, i - 2))
                    c["as_"].inc(s.activation(
                        out=ln1s[i % 2][:], in_=xst[i % 2][:], func=AF.Square,
                        bias=self.col(self.negm, i % 2),
                        accum_out=self.col(self.ssqa, i % 2)))
                    self.ev[("sq", b, i)] = c["as_"].n
                    self.wt(s, "scalar", c["as_"], c["as_"].n)
                    c["as_"].inc(s.activation(
                        out=self.col(self.t1c, i % 2),
                        in_=self.col(self.ssqa, i % 2), func=AF.Ln,
                        scale=1.0 / E, bias=self.col(self.cbias, 0)))
                    self.wt(s, "scalar", c["as_"], c["as_"].n)
                    c["as_"].inc(s.activation(
                        out=self.col(self.rstd, i % 2),
                        in_=self.col(self.t1c, i % 2), func=AF.Exp,
                        scale=-0.5))
                    self.ev[("rstd", b, i)] = c["as_"].n

            if i < SC:
                @self.blk.vector
                def _(v):
                    self.wte(v, "vector", c["as_"], ("rstd", b, i))
                    c["vs"].inc(v.tensor_scalar(
                        ln1s[i % 2][:], xst[i % 2][:],
                        self.col(self.negm, i % 2),
                        self.col(self.rstd, i % 2),
                        op0=ALU.add, op1=ALU.mult))
                self.ev[("ln1s", b, i)] = c["vs"].n

            @self.blk.tensor
            def _(t):
                if i - 1 >= 0:
                    j = i - 1
                    self.wte(t, "tensor", c["as_"], ("evT", b, j))
                    self.wte(t, "tensor", c["vs"], ("evQ", b, j - 1))
                    self.wte(t, "tensor", c["swq"], ("wqld", b))
                    for e in range(EC):
                        for sbk in range(4):
                            mm = t.matmul(
                                psQ[:, sbk * NB:(sbk + 1) * NB],
                                self.big0[:, e * E + j * P:
                                          e * E + j * P + P],
                                self.big2[:, e * E + sbk * NB:
                                          e * E + (sbk + 1) * NB],
                                start=(e == 0), stop=(e == EC - 1))
                    c["ts"].inc(mm)
                    self.ev[("Q", b, j)] = c["ts"].n
                if i < SC:
                    self.wte(t, "tensor", c["scst"], ("consts",))
                    self.wte(t, "tensor", c["vs"], ("ln1s", b, i))
                    self.wte(t, "tensor", c["as_"], ("evT", b, i - 1))
                    for tt in range(EC):
                        mm = t.transpose(
                            psT[:, tt * P:(tt + 1) * P],
                            ln1s[i % 2][:, tt * P:(tt + 1) * P],
                            self.ident[:])
                    c["ts"].inc(mm)
                    self.ev[("T", b, i)] = c["ts"].n

            if i - 1 >= 0:
                j = i - 1

                @self.blk.vector
                def _(v):
                    self.wte(v, "vector", c["ts"], ("Q", b, j))
                    self.wte(v, "vector", c[f"sqo{j % 2}"], ("qst", b, j - 2))
                    c["vs"].inc(v.tensor_copy(qstage[j % 2][:], psQ[:]))
                self.ev[("evQ", b, j)] = c["vs"].n

                @self.blk.gpsimd
                def _(g):
                    self.wte(g, "gpsimd", c["vs"], ("evQ", b, j))
                    dst = bass.AP(self.qsl.tensor, j * P,
                                  [[E, P], [BLK, EC], [1, P]])
                    c[f"sqo{j % 2}"].inc(g.dma_start(
                        out=dst,
                        in_=qstage[j % 2][:].rearrange(
                            "p (a i) -> p a i", i=P)), 16)
                self.ev[("qst", b, j)] = c[f"sqo{j % 2}"].n

            if i + 2 < SC:
                @self.blk.sync
                def _(sy):
                    self.wte(sy, "sync", c["as_"], ("sq", b, i))
                    self.wte(sy, "sync", c["vs"], ("ln1s", b, i))
                    c[f"sx{i % 2}"].inc(sy.dma_start(
                        out=xst[i % 2][:],
                        in_=self.x[b, (i + 2) * P:(i + 3) * P, :]), 16)
                self.ev[("xld", b, i + 2)] = c[f"sx{i % 2}"].n

            if b == 0:
                self.quant_step(1)
        sc.close()

    def phase_a1b(self, b):
        c, nc = self.c, self.nc
        sc = contextlib.ExitStack()
        wkh = sc.enter_context(
            nc.sbuf_tensor(f"k{b}_wkh", [P, EC * 1024], BF16))
        psK = [sc.enter_context(
            nc.psum_tensor(f"k{b}_psK{k}", [P, 1024], F32)) for k in range(2)]

        for h in range(2):
            @self.blk.gpsimd
            def _(g):
                if h == 0:
                    self.ldbar(g, "gpsimd")
                if h == 1:
                    self.wte(g, "gpsimd", c["ts"], ("K", b, 0, SC - 1))
                for cc in range(EC):
                    src = bass.AP(self.wk.tensor, cc * P * E + h * 1024,
                                  [[E, P], [1, 1024]])
                    c["swk"].inc(g.dma_start(
                        out=wkh[:, cc * 1024:(cc + 1) * 1024], in_=src), 16)
                    self.ev[("wkld", b, h, cc)] = c["swk"].n

            for i in range(SC):
                @self.blk.tensor
                def _(t):
                    self.wte(t, "tensor", c["swk"], ("wkld", b, h, EC - 1))
                    if i >= 2:
                        self.wte(t, "tensor", c["vs"], ("evK", b, h, i - 2))
                    elif h == 1:
                        self.wte(t, "tensor", c["vs"],
                                 ("evK", b, 0, SC + i - 2))
                    if h == 0 and i == 0:
                        self.wte(t, "tensor", c["as_"], ("evT", b, SC - 1))
                        self.wte(t, "tensor", c["ts"], ("Q", b, SC - 1))
                        self.wte(t, "tensor", c["vs"], ("evQ", b, SC - 1))
                    for cc in range(EC):
                        for sbk in range(2):
                            mm = t.matmul(
                                psK[i % 2][:, sbk * NB:(sbk + 1) * NB],
                                self.big0[:, cc * E + i * P:
                                          cc * E + i * P + P],
                                wkh[:, cc * 1024 + sbk * NB:
                                    cc * 1024 + (sbk + 1) * NB],
                                start=(cc == 0), stop=(cc == EC - 1))
                    c["ts"].inc(mm)
                    self.ev[("K", b, h, i)] = c["ts"].n

                @self.blk.vector
                def _(v):
                    self.wte(v, "vector", c["ts"], ("K", b, h, i))
                    c["vs"].inc(v.tensor_copy(
                        self.big2[:, i * E + h * 1024:
                                  i * E + h * 1024 + 1024],
                        psK[i % 2][:]))
                self.ev[("evK", b, h, i)] = c["vs"].n

                if b == 0:
                    self.quant_step(1)
        sc.close()

    def phase_a2(self, b):
        c, nc = self.c, self.nc
        sc = contextlib.ExitStack()
        wvb = [sc.enter_context(
            nc.sbuf_tensor(f"v{b}_wvb{k}", [P, E], BF16)) for k in range(2)]
        vstage = [sc.enter_context(
            nc.sbuf_tensor(f"v{b}_vst{k}", [P, E], BF16)) for k in range(2)]
        psV = [sc.enter_context(
            nc.psum_tensor(f"v{b}_psV{k}", [P, E], F32)) for k in range(2)]

        def wv_load(fc):
            @self.blk.gpsimd
            def _(g):
                if fc < 2:
                    self.ldbar(g, "gpsimd")
                self.wte(g, "gpsimd", c["ts"], ("V", b, fc - 2))
                src = bass.AP(self.wv.tensor, fc * P,
                              [[E, P], [P * E, EC], [1, P]])
                c[f"swv{fc % 2}"].inc(g.dma_start(out=wvb[fc % 2][:],
                                                  in_=src), 16)
            self.ev[("wvld", b, fc)] = c[f"swv{fc % 2}"].n

        wv_load(0)
        wv_load(1)
        for fc in range(EC):
            @self.blk.tensor
            def _(t):
                self.wte(t, "tensor", c[f"swv{fc % 2}"], ("wvld", b, fc))
                self.wte(t, "tensor", c["vs"], ("evV", b, fc - 2))
                if fc == 0:
                    self.wte(t, "tensor", c["vs"], ("evK", b, 1, SC - 1))
                for tt in range(EC):
                    for sbk in range(4):
                        mm = t.matmul(
                            psV[fc % 2][:, sbk * NB:(sbk + 1) * NB],
                            wvb[fc % 2][:, tt * P:(tt + 1) * P],
                            self.big0[:, tt * E + sbk * NB:
                                      tt * E + (sbk + 1) * NB],
                            start=(tt == 0), stop=(tt == EC - 1))
                c["ts"].inc(mm)
                self.ev[("V", b, fc)] = c["ts"].n

            @self.blk.vector
            def _(v):
                self.wte(v, "vector", c["ts"], ("V", b, fc))
                self.wte(v, "vector", c[f"svo{fc % 2}"], ("vst", b, fc - 2))
                c["vs"].inc(v.tensor_copy(vstage[fc % 2][:], psV[fc % 2][:]))
            self.ev[("evV", b, fc)] = c["vs"].n

            @self.blk.gpsimd
            def _(g):
                self.wte(g, "gpsimd", c["vs"], ("evV", b, fc))
                dst = bass.AP(self.vfs.tensor, fc * BLK, [[E, P], [1, E]])
                c[f"svo{fc % 2}"].inc(g.dma_start(out=dst,
                                                  in_=vstage[fc % 2][:]), 16)
            self.ev[("vst", b, fc)] = c[f"svo{fc % 2}"].n

            if fc + 2 < EC:
                wv_load(fc + 2)
            if b == 0:
                self.quant_step(3)
                self.transp1_step(4)
        if b == 0:
            self.quant_step(128)
            self.transp1_step(EC * 4)
        sc.close()

    def phase_b1(self, b):
        c, nc = self.c, self.nc
        sc = contextlib.ExitStack()
        qin = [sc.enter_context(
            nc.sbuf_tensor(f"b1{b}_qin{k}", [P, E], BF16)) for k in range(2)]
        psb = [sc.enter_context(
            nc.sbuf_tensor(f"b1{b}_psb{k}", [P, E], BF16)) for k in range(2)]
        psS = sc.enter_context(nc.psum_tensor(f"b1{b}_psS", [P, E], F32))
        psTb = sc.enter_context(nc.psum_tensor(f"b1{b}_psTb", [P, E], BF16))

        def q_load(j):
            @self.blk.sync
            def _(sy):
                if j < 2:
                    self.ldbar(sy, "sync")
                self.wt(sy, "sync", c["sqo0"], c["sqo0"].n)
                self.wt(sy, "sync", c["sqo1"], c["sqo1"].n)
                self.wte(sy, "sync", c["ts"], ("S", b, j - 2))
                src = bass.AP(self.qsl.tensor, j * BLK, [[E, P], [1, E]])
                c[f"sqi{j % 2}"].inc(sy.dma_start(out=qin[j % 2][:],
                                                  in_=src), 16)
            self.ev[("qld2", b, j)] = c[f"sqi{j % 2}"].n

        q_load(0)
        q_load(1)
        for j in range(SC + 1):
            @self.blk.tensor
            def _(t):
                if j < SC:
                    self.wte(t, "tensor", c[f"sqi{j % 2}"], ("qld2", b, j))
                    self.wte(t, "tensor", c["as_"], ("Exp", b, j - 1))
                    if j == 0:
                        self.wte(t, "tensor", c["vs"],
                                 ("evK", b, 1, SC - 1))
                        self.wte(t, "tensor", c["vs"], ("evV", b, EC - 1))
                    for cc in range(SC):
                        for sbk in range(4):
                            mm = t.matmul(
                                psS[:, sbk * NB:(sbk + 1) * NB],
                                qin[j % 2][:, cc * P:(cc + 1) * P],
                                self.big2[:, cc * E + sbk * NB:
                                          cc * E + (sbk + 1) * NB],
                                start=(cc == 0), stop=(cc == SC - 1))
                    c["ts"].inc(mm)
                    self.ev[("S", b, j)] = c["ts"].n
                if j - 1 >= 0:
                    jj = j - 1
                    self.wte(t, "tensor", c["as_"], ("Exp", b, jj))
                    self.wte(t, "tensor", c["as_"], ("evTb", b, jj - 1))
                    for tt in range(EC):
                        mm = t.transpose(
                            psTb[:, tt * P:(tt + 1) * P],
                            psb[jj % 2][:, tt * P:(tt + 1) * P],
                            self.ident[:])
                    c["ts"].inc(mm)
                    self.ev[("Tb", b, jj)] = c["ts"].n

            if j < SC:
                @self.blk.vector
                def _(v):
                    self.wte(v, "vector", c["ts"], ("S", b, j))
                    self.wte(v, "vector", c["as_"], ("Exp", b, j - 2))
                    c["vs"].inc(v.reduce_max(
                        self.col(self.mxc, j % 2), psS[:], axis=AX))
                    self.wt(v, "vector", c["vs"], c["vs"].n)
                    c["vs"].inc(v.tensor_scalar_mul(
                        self.col(self.nmx8, j % 2),
                        self.col(self.mxc, j % 2), -0.125))
                self.ev[("nmx8", b, j)] = c["vs"].n

                @self.blk.scalar
                def _(s):
                    self.wte(s, "scalar", c["vs"], ("nmx8", b, j))
                    self.wte(s, "scalar", c["ts"], ("Tb", b, j - 2))
                    c["as_"].inc(s.activation(
                        out=psb[j % 2][:], in_=psS[:], func=AF.Exp,
                        scale=0.125, bias=self.col(self.nmx8, j % 2),
                        accum_out=self.col(self.srow, j % 2)))
                self.ev[("Exp", b, j)] = c["as_"].n

                @self.blk.vector
                def _(v):
                    self.wte(v, "vector", c["as_"], ("Exp", b, j))
                    c["vs"].inc(v.reciprocal(
                        self.rowrec[:, j:j + 1], self.col(self.srow, j % 2)))
                self.ev[("rr", b, j)] = c["vs"].n

                if self.debug and b == 1 and j == 0:
                    @self.blk.gpsimd
                    def _(g):
                        self.wte(g, "gpsimd", c["as_"], ("Exp", b, j))
                        c["sout"].inc(g.dma_start(
                            out=self.dbg3, in_=psb[0][:]), 16)
                        self.wte(g, "gpsimd", c[f"sqi{0}"], ("qld2", b, 0))
                        c["sout"].inc(g.dma_start(
                            out=self.dbg4, in_=qin[0][:]), 16)

            if j - 1 >= 0:
                jj = j - 1

                @self.blk.scalar
                def _(s):
                    self.wte(s, "scalar", c["ts"], ("Tb", b, jj))
                    c["as_"].inc(s.copy(
                        self.big0[:, jj * E:(jj + 1) * E], psTb[:]))
                self.ev[("evTb", b, jj)] = c["as_"].n

            if j + 2 < SC:
                q_load(j + 2)
            if b == 0:
                self.quant_step(2)
                self.transp2_step(2)
        sc.close()

    def phase_b2(self, b):
        c, nc = self.c, self.nc
        sc = contextlib.ExitStack()
        psO = [sc.enter_context(
            nc.psum_tensor(f"b2{b}_psO{k}", [P, E], F32)) for k in range(2)]

        @self.blk.sync
        def _(sy):
            self.wte(sy, "sync", c["ts"], ("S", b, SC - 1))
            self.wt(sy, "sync", c["svo0"], c["svo0"].n)
            self.wt(sy, "sync", c["svo1"], c["svo1"].n)
            for cc in range(EC):
                sv = c[("svia", "svib", "svic", "svid")[cc // 4]]
                src = bass.AP(self.vfs.tensor, cc * BLK, [[E, P], [1, E]])
                sv.inc(sy.dma_start(
                    out=self.big2[:, cc * E:(cc + 1) * E], in_=src), 16)
                self.ev[("vfg", b, cc // 4)] = sv.n

        for j in range(SC):
            @self.blk.tensor
            def _(t):
                self.wte(t, "tensor", c["vs"], ("evO", b, j - 2))
                if j == 0:
                    self.wte(t, "tensor", c["as_"], ("evTb", b, SC - 1))
                    self.wte(t, "tensor", c["vs"], ("rr", b, SC - 1))
                for tt in range(EC):
                    self.wte(t, "tensor",
                             c[("svia", "svib", "svic", "svid")[tt // 4]],
                             ("vfg", b, tt // 4))
                    for sbk in range(4):
                        mm = t.matmul(
                            psO[j % 2][:, sbk * NB:(sbk + 1) * NB],
                            self.big0[:, j * E + tt * P: j * E + tt * P + P],
                            self.big2[:, tt * E + sbk * NB:
                                      tt * E + (sbk + 1) * NB],
                            start=(tt == 0), stop=(tt == EC - 1))
                c["ts"].inc(mm)
                self.ev[("O", b, j)] = c["ts"].n

            @self.blk.vector
            def _(v):
                self.wte(v, "vector", c["ts"], ("O", b, j))
                c["vs"].inc(v.tensor_single_scalar(
                    self.big0[:, j * E:(j + 1) * E], psO[j % 2][:],
                    self.rowrec[:, j:j + 1], op=ALU.mult))
            self.ev[("evO", b, j)] = c["vs"].n

            if b == 0:
                self.quant_step(2)
                self.transp2_step(2)
        if b == 0:
            self.quant_step(128)
            self.transp2_step(HC)

        @self.blk.gpsimd
        def _(g):
            self.wte(g, "gpsimd", c["ts"], ("O", b, SC - 1))
            if b == 0:
                self.wte(g, "gpsimd", c["stro"], ("w2T_done",))
            for cc in range(EC):
                sw = c[("swoa", "swob", "swoc", "swod")[cc // 4]]
                ins = g.dma_start(
                    out=self.big2[:, cc * E:(cc + 1) * E],
                    in_=self.wo[cc * P:(cc + 1) * P, :])
                sw.inc(ins, 16)
                self.ev[("wog", b, cc // 4)] = sw.n
        sc.close()
        if b == 0:
            self.trscope.close()
            self.qscope.close()

    def phase_c(self, b):
        c, nc = self.c, self.nc
        sc = contextlib.ExitStack()
        ystage = sc.enter_context(nc.sbuf_tensor(f"c{b}_yst", [P, E], F32))
        ln2s = [sc.enter_context(
            nc.sbuf_tensor(f"c{b}_ln2s{k}", [P, E], BF16)) for k in range(2)]
        l2stage = sc.enter_context(nc.sbuf_tensor(f"c{b}_l2st", [P, E], BF16))
        psC = sc.enter_context(nc.psum_tensor(f"c{b}_psC", [P, E], F32))
        psTc = sc.enter_context(nc.psum_tensor(f"c{b}_psTc", [P, E], BF16))

        for i in range(SC + 1):
            @self.blk.tensor
            def _(t):
                if i < SC:
                    self.wte(t, "tensor", c["vs"], ("ap1", b, i - 1))
                    if i == 0:
                        self.wte(t, "tensor", c["vs"], ("evO", b, SC - 1))
                    for tt in range(EC):
                        self.wte(t, "tensor",
                                 c[("swoa", "swob", "swoc", "swod")[tt // 4]],
                                 ("wog", b, tt // 4))
                        for sbk in range(4):
                            mm = t.matmul(
                                psC[:, sbk * NB:(sbk + 1) * NB],
                                self.big0[:, tt * E + i * P:
                                          tt * E + i * P + P],
                                self.big2[:, tt * E + sbk * NB:
                                          tt * E + (sbk + 1) * NB],
                                start=(tt == 0), stop=(tt == EC - 1))
                    c["ts"].inc(mm)
                    self.ev[("PRJ", b, i)] = c["ts"].n
                if i - 1 >= 0:
                    ii = i - 1
                    self.wte(t, "tensor", c["vs"], ("ln2s", b, ii))
                    self.wte(t, "tensor", c["as_"], ("evTc", b, ii - 1))
                    for tt in range(EC):
                        mm = t.transpose(
                            psTc[:, tt * P:(tt + 1) * P],
                            ln2s[ii % 2][:, tt * P:(tt + 1) * P],
                            self.ident[:])
                    c["ts"].inc(mm)
                    self.ev[("Tc", b, ii)] = c["ts"].n

            if i < SC:
                @self.blk.vector
                def _(v):
                    self.wte(v, "vector", c["ts"], ("PRJ", b, i))
                    self.wte(v, "vector", c["as_"], ("sqC", b, i - 2))
                    c["vs"].inc(v.reduce_sum(
                        self.col(self.tA, i % 2), psC[:], axis=AX))
                    self.wt(v, "vector", c["vs"], c["vs"].n)
                    c["vs"].inc(v.tensor_scalar_mul(
                        self.col(self.negm, i % 2),
                        self.col(self.tA, i % 2), -1.0 / E))
                self.ev[("negmC", b, i)] = c["vs"].n

                @self.blk.scalar
                def _(s):
                    self.wte(s, "scalar", c["vs"], ("negmC", b, i))
                    self.wte(s, "scalar", c["ts"], ("Tc", b, i - 2))
                    c["as_"].inc(s.activation(
                        out=ln2s[i % 2][:], in_=psC[:], func=AF.Square,
                        bias=self.col(self.negm, i % 2),
                        accum_out=self.col(self.ssqa, i % 2)))
                    self.ev[("sqC", b, i)] = c["as_"].n
                    self.wt(s, "scalar", c["as_"], c["as_"].n)
                    c["as_"].inc(s.activation(
                        out=self.col(self.t1c, i % 2),
                        in_=self.col(self.ssqa, i % 2), func=AF.Ln,
                        scale=1.0 / E, bias=self.col(self.cbias, 0)))
                    self.wt(s, "scalar", c["as_"], c["as_"].n)
                    c["as_"].inc(s.activation(
                        out=self.col(self.rstd, i % 2),
                        in_=self.col(self.t1c, i % 2), func=AF.Exp,
                        scale=-0.5))
                    self.ev[("rstdC", b, i)] = c["as_"].n

                @self.blk.vector
                def _(v):
                    self.wte(v, "vector", c["as_"], ("rstdC", b, i))
                    self.wte(v, "vector", c["syo"], ("yst", b, i - 1))
                    c["vs"].inc(v.tensor_scalar(
                        ystage[:], psC[:],
                        self.col(self.negm, i % 2),
                        self.col(self.rstd, i % 2),
                        op0=ALU.add, op1=ALU.mult))
                self.ev[("ap1", b, i)] = c["vs"].n

                @self.blk.gpsimd
                def _(g):
                    self.wte(g, "gpsimd", c["vs"], ("ap1", b, i))
                    c["sxc"].inc(g.dma_start(
                        out=ystage[:], in_=self.x[b, i * P:(i + 1) * P, :],
                        accum_op=ALU.add), 16)
                self.ev[("xacc", b, i)] = c["sxc"].n

                @self.blk.vector
                def _(v):
                    self.wte(v, "vector", c["sxc"], ("xacc", b, i))
                    c["vs"].inc(v.reduce_sum(
                        self.col(self.tB, i % 2), ystage[:], axis=AX))
                    self.wt(v, "vector", c["vs"], c["vs"].n)
                    c["vs"].inc(v.tensor_scalar_mul(
                        self.col(self.negm2, i % 2),
                        self.col(self.tB, i % 2), -1.0 / E))
                self.ev[("negm2", b, i)] = c["vs"].n

                @self.blk.gpsimd
                def _(g):
                    self.wte(g, "gpsimd", c["sxc"], ("xacc", b, i))
                    dst = bass.AP(self.ys.tensor, i * BLK, [[E, P], [1, E]])
                    c["syo"].inc(g.dma_start(out=dst, in_=ystage[:]), 16)
                self.ev[("yst", b, i)] = c["syo"].n

                @self.blk.scalar
                def _(s):
                    self.wte(s, "scalar", c["vs"], ("negm2", b, i))
                    c["as_"].inc(s.activation(
                        out=ln2s[i % 2][:], in_=ystage[:], func=AF.Square,
                        bias=self.col(self.negm2, i % 2),
                        accum_out=self.col(self.ssqb, i % 2)))
                    self.wt(s, "scalar", c["as_"], c["as_"].n)
                    c["as_"].inc(s.activation(
                        out=self.col(self.t1c, i % 2),
                        in_=self.col(self.ssqb, i % 2), func=AF.Ln,
                        scale=1.0 / E, bias=self.col(self.cbias, 0)))
                    self.wt(s, "scalar", c["as_"], c["as_"].n)
                    c["as_"].inc(s.activation(
                        out=self.col(self.rstd2, i % 2),
                        in_=self.col(self.t1c, i % 2), func=AF.Exp,
                        scale=-0.5))
                    self.ev[("rstd2", b, i)] = c["as_"].n

                @self.blk.vector
                def _(v):
                    self.wte(v, "vector", c["as_"], ("rstd2", b, i))
                    c["vs"].inc(v.tensor_scalar(
                        ln2s[i % 2][:], ystage[:],
                        self.col(self.negm2, i % 2),
                        self.col(self.rstd2, i % 2),
                        op0=ALU.add, op1=ALU.mult))
                self.ev[("ln2s", b, i)] = c["vs"].n

            if i - 1 >= 0:
                ii = i - 1

                @self.blk.scalar
                def _(s):
                    self.wte(s, "scalar", c["ts"], ("Tc", b, ii))
                    self.wte(s, "scalar", c["sl2o"], ("l2st", b, ii - 1))
                    c["as_"].inc(s.copy(l2stage[:], psTc[:]))
                self.ev[("evTc", b, ii)] = c["as_"].n

                @self.blk.gpsimd
                def _(g):
                    self.wte(g, "gpsimd", c["as_"], ("evTc", b, ii))
                    dst = bass.AP(self.ln2d.tensor, ii * P,
                                  [[E, P], [BLK, EC], [1, P]])
                    c["sl2o"].inc(g.dma_start(
                        out=dst,
                        in_=l2stage[:].rearrange("p (t r) -> p t r", r=P)),
                        16)
                self.ev[("l2st", b, ii)] = c["sl2o"].n
        sc.close()

    def phase_d(self, b):
        c, nc = self.c, self.nc
        sc = contextlib.ExitStack()
        ssb = lambda n, s, dt: sc.enter_context(nc.sbuf_tensor(n, s, dt))
        ln2fg = [ssb(f"d{b}_l2g{k}", [P, EC * GTOK], BF16) for k in range(2)]
        wtile = [ssb(f"d{b}_wt{k}", [P, E], BF16) for k in range(2)]
        w2sl = [ssb(f"d{b}_w2{k}", [P, 1024], BF16) for k in range(2)]
        projsb = [ssb(f"d{b}_pj{k}", [P, E], BF16) for k in range(GSTR)]
        rstage = [ssb(f"d{b}_rs{k}", [P, NB], F32) for k in range(2)]
        g2st = [ssb(f"d{b}_g2{k}", [P, NB], BF16) for k in range(2)]
        outstage = ssb(f"d{b}_outst", [P, E], F32)

        def l2g_load(g_):
            @self.blk.sync
            def _(sy):
                if g_ == 0:
                    self.ldbar(sy, "sync")
                self.wte(sy, "sync", c["sl2o"], ("l2st", b, SC - 1))
                self.wte(sy, "sync", c["ts"], ("fc", b, g_ - 2, HC - 1))
                for tt in range(EC):
                    src = bass.AP(self.ln2d.tensor, tt * BLK + g_ * GTOK,
                                  [[E, P], [1, GTOK]])
                    c["sl2i"].inc(sy.dma_start(
                        out=ln2fg[g_ % 2][:, tt * GTOK:(tt + 1) * GTOK],
                        in_=src), 16)
            self.ev[("l2ld", b, g_)] = c["sl2i"].n

        def w1_load(g_, jc):
            @self.blk.sync
            def _(sy):
                if g_ == 0 and jc < 2:
                    self.ldbar(sy, "sync")
                self.wte(sy, "sync", c["stro"], ("w1T_done",))
                if jc >= 2:
                    self.wte(sy, "sync", c["ts"], ("fc", b, g_, jc - 2))
                else:
                    self.wte(sy, "sync", c["ts"],
                             ("fc", b, g_ - 1, HC + jc - 2))
                src = bass.AP(self.w1qT_d.tensor, jc * BLK, [[E, P], [1, E]])
                sw = c["sw1a"] if jc % 2 == 0 else c["sw1b"]
                sw.inc(sy.dma_start(out=wtile[jc % 2][:], in_=src), 16)
            sw = c["sw1a"] if jc % 2 == 0 else c["sw1b"]
            self.ev[("w1ld", b, g_, jc)] = sw.n

        l2g_load(0)
        for g_ in range(NGRP):
            scF = contextlib.ExitStack()
            psF = [scF.enter_context(
                nc.psum_tensor(f"d{b}_{g_}_psF{k}", [P, NB], F32))
                for k in range(2)]
            psR = scF.enter_context(
                nc.psum_tensor(f"d{b}_{g_}_psR", [P, NB], F32))

            w1_load(g_, 0)
            w1_load(g_, 1)
            for jc in range(HC):
                @self.blk.tensor
                def _(t):
                    sw = c["sw1a"] if jc % 2 == 0 else c["sw1b"]
                    self.wte(t, "tensor", sw, ("w1ld", b, g_, jc))
                    self.wte(t, "tensor", c["sl2i"], ("l2ld", b, g_))
                    self.wte(t, "tensor", c["as_"], ("relu", b, g_, jc - 2))
                    if g_ == 0 and jc == 0:
                        self.wte(t, "tensor", c["vs"], ("ln2s", b, SC - 1))
                    if jc == 0:
                        self.wte(t, "tensor", c["ts"],
                                 ("proj", b, g_ - 1, 1, HC - 1))
                        for st in range(GSTR):
                            self.wte(t, "tensor", c["vs"],
                                     ("evP", b, g_ - 1, 1, st))
                        if g_ == 0:
                            self.wte(t, "tensor", c["ts"], ("PRJ", b, SC - 1))
                    for dc in range(EC):
                        mm = t.matmul(
                            psF[jc % 2][:, 0:NB],
                            wtile[jc % 2][:, dc * P:(dc + 1) * P],
                            ln2fg[g_ % 2][:, dc * GTOK:dc * GTOK + NB],
                            start=(dc == 0), stop=(dc == EC - 1))
                    c["ts"].inc(mm)
                    self.ev[("fc", b, g_, jc)] = c["ts"].n
                    if jc - 2 >= 0:
                        self.wte(t, "tensor", c["as_"],
                                 ("g2", b, g_, jc - 2))
                        self.wte(t, "tensor", c["scst"], ("consts",))
                        mm = t.matmul(
                            psR[0:1, 0:NB], self.ones[:],
                            g2st[(jc - 2) % 2][:],
                            start=(jc - 2 == 0), stop=(jc - 2 == HC - 1))
                        c["ts"].inc(mm)
                        self.ev[("strow", b, g_, jc - 2)] = c["ts"].n

                @self.blk.scalar
                def _(s):
                    self.wte(s, "scalar", c["ts"], ("fc", b, g_, jc))
                    self.wte(s, "scalar", c["vs"], ("gg", b, g_, jc - 2))
                    c["as_"].inc(s.activation(
                        out=rstage[jc % 2][:], in_=psF[jc % 2][:, 0:NB],
                        func=AF.Relu))
                self.ev[("relu", b, g_, jc)] = c["as_"].n

                @self.blk.vector
                def _(v):
                    self.wte(v, "vector", c["as_"], ("relu", b, g_, jc))
                    c["vs"].inc(v.tensor_mul(
                        self.big0[:, jc * NB:(jc + 1) * NB],
                        rstage[jc % 2][:], rstage[jc % 2][:]))
                self.ev[("gg", b, g_, jc)] = c["vs"].n

                @self.blk.scalar
                def _(s):
                    self.wte(s, "scalar", c["vs"], ("gg", b, g_, jc))
                    self.wte(s, "scalar", c["ts"], ("strow", b, g_, jc - 2))
                    c["as_"].inc(s.activation(
                        out=g2st[jc % 2][:],
                        in_=self.big0[:, jc * NB:(jc + 1) * NB],
                        func=AF.Square))
                self.ev[("g2", b, g_, jc)] = c["as_"].n

                if jc + 2 < HC:
                    w1_load(g_, jc + 2)
                elif jc == HC - 1 and g_ + 1 < NGRP:
                    l2g_load(g_ + 1)

            @self.blk.tensor
            def _(t):
                for jt in (HC - 2, HC - 1):
                    self.wte(t, "tensor", c["as_"], ("g2", b, g_, jt))
                    mm = t.matmul(
                        psR[0:1, 0:NB], self.ones[:], g2st[jt % 2][:],
                        start=(jt == 0), stop=(jt == HC - 1))
                    c["ts"].inc(mm)
                    self.ev[("strow", b, g_, jt)] = c["ts"].n

            @self.blk.vector
            def _(v):
                self.wte(v, "vector", c["ts"], ("strow", b, g_, HC - 1))
                c["vs"].inc(v.tensor_copy(self.ssqrow[0:1, :], psR[0:1, :]))
            self.ev[("ssqrow", b, g_)] = c["vs"].n

            @self.blk.scalar
            def _(s):
                self.wte(s, "scalar", c["vs"], ("ssqrow", b, g_))
                c["as_"].inc(s.activation(
                    out=self.lamrow[0:1, :], in_=self.ssqrow[0:1, :],
                    func=AF.Ln, scale=1.0 / H, bias=self.cbias[0:1, 1:2]))
                self.wt(s, "scalar", c["as_"], c["as_"].n)
                c["as_"].inc(s.activation(
                    out=self.lamrow[0:1, :], in_=self.lamrow[0:1, :],
                    func=AF.Exp, scale=-0.5))
            self.ev[("lrow", b, g_)] = c["as_"].n

            @self.blk.gpsimd
            def _(g):
                self.wte(g, "gpsimd", c["as_"], ("lrow", b, g_))
                dst = bass.AP(self.rowd.tensor, 0, [[1, 1], [1, GTOK]])
                c["srb"].inc(g.dma_start(out=dst, in_=self.lamrow[0:1, :]),
                             16)
            self.ev[("rbo", b, g_)] = c["srb"].n

            @self.blk.sync
            def _(sy):
                self.wte(sy, "sync", c["srb"], ("rbo", b, g_))
                src = bass.AP(self.rowd.tensor, 0, [[1, P], [P, GSTR]])
                c["srb"].inc(sy.dma_start(
                    out=self.lamcol[:], in_=src,
                    allow_slow_non_contiguous=True), 16)
            self.ev[("lam", b, g_)] = c["srb"].n

            if self.phases == "Dfc":
                scF.close()
                break
            scF.close()
            scP = contextlib.ExitStack()
            psP = [scP.enter_context(
                nc.psum_tensor(f"d{b}_{g_}_psP{k}", [P, 1024], F32))
                for k in range(GSTR)]

            def w2_load(dbh, jc):
                @self.blk.sync
                def _(sy):
                    if g_ == 0 and dbh == 0 and jc < 2:
                        self.ldbar(sy, "sync")
                    self.wte(sy, "sync", c["stro"], ("w2T_done",))
                    if jc >= 2:
                        self.wte(sy, "sync", c["ts"],
                                 ("proj", b, g_, dbh, jc - 2))
                    elif dbh == 1:
                        self.wte(sy, "sync", c["ts"],
                                 ("proj", b, g_, 0, HC + jc - 2))
                    else:
                        self.wte(sy, "sync", c["ts"],
                                 ("proj", b, g_ - 1, 1, HC + jc - 2))
                    src = bass.AP(self.w2T_d.tensor, jc * BLK + dbh * 1024,
                                  [[E, P], [1, 1024]])
                    sw = c["sw2a"] if jc % 2 == 0 else c["sw2b"]
                    sw.inc(sy.dma_start(out=w2sl[jc % 2][:], in_=src), 16)
                sw = c["sw2a"] if jc % 2 == 0 else c["sw2b"]
                self.ev[("w2ld", b, g_, dbh, jc)] = sw.n

            for dbh in range(2):
                w2_load(dbh, 0)
                w2_load(dbh, 1)
                for jc in range(HC):
                    @self.blk.tensor
                    def _(t):
                        sw = c["sw2a"] if jc % 2 == 0 else c["sw2b"]
                        self.wte(t, "tensor", sw, ("w2ld", b, g_, dbh, jc))
                        self.wte(t, "tensor", c["as_"],
                                 ("g2", b, g_, HC - 1))
                        if jc == 0:
                            if dbh == 0:
                                self.wte(t, "tensor", c["as_"],
                                         ("relu", b, g_, HC - 1))
                                self.wte(t, "tensor", c["vs"],
                                         ("ssqrow", b, g_))
                            else:
                                for st in range(GSTR):
                                    self.wte(t, "tensor", c["vs"],
                                             ("evP", b, g_, 0, st))
                        for st in range(GSTR):
                            for d2 in range(2):
                                mm = t.matmul(
                                    psP[st][:, d2 * NB:(d2 + 1) * NB],
                                    self.big0[:, jc * NB + st * P:
                                              jc * NB + st * P + P],
                                    w2sl[jc % 2][:, d2 * NB:(d2 + 1) * NB],
                                    start=(jc == 0), stop=(jc == HC - 1))
                        c["ts"].inc(mm)
                        self.ev[("proj", b, g_, dbh, jc)] = c["ts"].n
                    if jc + 2 < HC:
                        w2_load(dbh, jc + 2)

                for st in range(GSTR):
                    @self.blk.vector
                    def _(v):
                        self.wte(v, "vector", c["ts"],
                                 ("proj", b, g_, dbh, HC - 1))
                        if dbh == 0:
                            self.wte(v, "vector", c["sout"],
                                     ("outst", b, g_ - 1, st))
                        c["vs"].inc(v.tensor_copy(
                            projsb[st][:, dbh * 1024:(dbh + 1) * 1024],
                            psP[st][:]))
                    self.ev[("evP", b, g_, dbh, st)] = c["vs"].n

            for st in range(GSTR):
                gi = g_ * GSTR + st

                @self.blk.vector
                def _(v):
                    self.wte(v, "vector", c["vs"], ("evP", b, g_, 1, st))
                    self.wte(v, "vector", c["as_"], ("sqD", b, g_, st - 2))
                    c["vs"].inc(v.reduce_sum(
                        self.col(self.tA, st % 2), projsb[st][:], axis=AX))
                    self.wt(v, "vector", c["vs"], c["vs"].n)
                    c["vs"].inc(v.tensor_scalar_mul(
                        self.col(self.negm3, st % 2),
                        self.col(self.tA, st % 2), -1.0 / E))
                self.ev[("negm3", b, g_, st)] = c["vs"].n

                @self.blk.scalar
                def _(s):
                    self.wte(s, "scalar", c["vs"], ("negm3", b, g_, st))
                    key = (("outst", b, g_, st - 1) if st > 0 else
                           ("outst", b, g_ - 1, GSTR - 1))
                    self.wte(s, "scalar", c["sout"], key)
                    c["as_"].inc(s.activation(
                        out=outstage[:], in_=projsb[st][:], func=AF.Square,
                        bias=self.col(self.negm3, st % 2),
                        accum_out=self.col(self.ssqa, st % 2)))
                self.ev[("sqD", b, g_, st)] = c["as_"].n

                @self.blk.vector
                def _(v):
                    self.wte(v, "vector", c["as_"], ("sqD", b, g_, st))
                    self.wte(v, "vector", c["srb"], ("lam", b, g_))
                    c["vs"].inc(v.tensor_scalar_mul(
                        self.col(self.tA, st % 2),
                        self.col(self.ssqa, st % 2), 1.0 / E))
                    self.wt(v, "vector", c["vs"], c["vs"].n)
                    c["vs"].inc(v.tensor_mul(
                        self.col(self.tB, st % 2),
                        self.lamcol[:, st:st + 1],
                        self.lamcol[:, st:st + 1]))
                    self.wt(v, "vector", c["vs"], c["vs"].n)
                    c["vs"].inc(v.tensor_mul(
                        self.col(self.tA, st % 2),
                        self.col(self.tA, st % 2),
                        self.col(self.tB, st % 2)))
                self.ev[("varD", b, g_, st)] = c["vs"].n

                @self.blk.scalar
                def _(s):
                    self.wte(s, "scalar", c["vs"], ("varD", b, g_, st))
                    c["as_"].inc(s.activation(
                        out=self.col(self.t1c, st % 2),
                        in_=self.col(self.tA, st % 2), func=AF.Ln,
                        bias=self.col(self.cbias, 0)))
                    self.wt(s, "scalar", c["as_"], c["as_"].n)
                    c["as_"].inc(s.activation(
                        out=self.col(self.rstd, st % 2),
                        in_=self.col(self.t1c, st % 2), func=AF.Exp,
                        scale=-0.5))
                self.ev[("rsD", b, g_, st)] = c["as_"].n

                @self.blk.vector
                def _(v):
                    self.wte(v, "vector", c["as_"], ("rsD", b, g_, st))
                    c["vs"].inc(v.tensor_mul(
                        self.col(self.lam, st % 2),
                        self.col(self.rstd, st % 2),
                        self.lamcol[:, st:st + 1]))
                    self.wt(v, "vector", c["vs"], c["vs"].n)
                    c["vs"].inc(v.tensor_scalar(
                        outstage[:], projsb[st][:],
                        self.col(self.negm3, st % 2),
                        self.col(self.lam, st % 2),
                        op0=ALU.add, op1=ALU.mult))
                self.ev[("outpre", b, g_, st)] = c["vs"].n

                @self.blk.gpsimd
                def _(g):
                    self.wte(g, "gpsimd", c["vs"], ("outpre", b, g_, st))
                    src = bass.AP(self.ys.tensor, gi * BLK, [[E, P], [1, E]])
                    c["sya"].inc(g.dma_start(
                        out=outstage[:], in_=src, accum_op=ALU.add), 16)
                    self.ev[("yacc", b, g_, st)] = c["sya"].n
                    self.wt(g, "gpsimd", c["sya"], c["sya"].n)
                    c["sout"].inc(g.dma_start(
                        out=self.out[b, gi * P:(gi + 1) * P, :],
                        in_=outstage[:]), 16)
                self.ev[("outst", b, g_, st)] = c["sout"].n

            scP.close()
            if self.phases == "Dg0":
                break
            if self.phases == "Dg1" and g_ == 1:
                break
            if self.phases == "Dg2" and g_ == 2:
                break
            if g_ == NGRP - 1 and b + 1 < NBATCH:
                @self.blk.gpsimd
                def _(g):
                    self.wte(g, "gpsimd", c["ts"], ("PRJ", b, SC - 1))
                    for cch in range(EC):
                        c["swq"].inc(g.dma_start(
                            out=self.big2[:, cch * E:(cch + 1) * E],
                            in_=self.wq[cch * P:(cch + 1) * P, :]), 16)
                self.ev[("wqld", b + 1)] = c["swq"].n
        self.ev[("outdone", b)] = c["sout"].n
        sc.close()

    def finale(self):
        c = self.c

        @self.blk.sync
        def _(sy):
            for n in ("sout", "syo", "sqo0", "sqo1", "svo0", "svo1",
                      "sqs0", "sqs1", "stro", "sl2o", "sya", "sxc", "srb"):
                self.wt(sy, "sync", c[n], c[n].n)

        if self.debug:
            @self.blk.sync
            def _(sy):
                for n, v in self.barrier_vals().items():
                    self.wt(sy, "sync", c[n], v)
                sy.dma_start(out=self.dbg0, in_=self.big0[:]).then_inc(
                    c["sout"].h, 16)
                sy.dma_start(out=self.dbg2, in_=self.big2[:]).then_inc(
                    c["sout"].h, 16)
                sy.wait_ge(c["sout"].h, c["sout"].n + 32)


_NC_CACHE = {}


def _get_nc(debug=None, phases=None):
    key = (debug, phases)
    if key not in _NC_CACHE:
        _NC_CACHE[key] = MK(debug=debug, phases=phases).build()
    return _NC_CACHE[key]


def _in_maps(x, wq, wk, wv, wo, wfc, wproj, ncores=8):
    x = np.ascontiguousarray(np.asarray(x, dtype=np.float32))
    ident = np.eye(P, dtype=ml_dtypes.bfloat16)
    ones = np.ones((P, 1), dtype=ml_dtypes.bfloat16)
    cbias = np.tile(np.array([[EPS_LN, EPS_RMS]], dtype=np.float32), (P, 1))
    mk = lambda a: np.ascontiguousarray(np.asarray(a, dtype=np.float32))
    maps = []
    for cidx in range(ncores):
        maps.append({
            "x": x[cidx * NBATCH:(cidx + 1) * NBATCH],
            "wq": mk(wq), "wk": mk(wk), "wv": mk(wv), "wo": mk(wo),
            "wfc": mk(wfc), "wproj": mk(wproj),
            "ident_in": ident, "ones_in": ones, "cbias_in": cbias,
        })
    return maps


def kernel(L0_attn_x, L0_attn_Wq, L0_attn_Wk, L0_attn_Wv, L0_attn_Wo,
           L0_ffn_W_fc, L0_ffn_W_proj):
    nc = _get_nc()
    maps = _in_maps(L0_attn_x, L0_attn_Wq, L0_attn_Wk, L0_attn_Wv,
                    L0_attn_Wo, L0_ffn_W_fc, L0_ffn_W_proj)
    res = run_bass_kernel_spmd(nc, maps, core_ids=list(range(8)))
    out = np.concatenate([np.asarray(r["out"]) for r in res.results], axis=0)
    return out.astype(np.float32)


# version 04
# speedup vs baseline: 1.0356x; 1.0326x over previous
import contextlib
import numpy as np
import ml_dtypes

import concourse.bass as bass
import concourse.mybir as mybir
from concourse.bass_utils import run_bass_kernel_spmd

F32 = mybir.dt.float32
BF16 = mybir.dt.bfloat16
AF = mybir.ActivationFunctionType
ALU = mybir.AluOpType
AX = mybir.AxisListType.X

P = 128
S = 2048
E = 2048
H = 8192
NB = 512
SC = S // P
EC = E // P
HC = H // P
NBATCH = 2
GTOK = 512
NGRP = S // GTOK
GSTR = GTOK // P
BLK = P * E

EPS_LN = 1e-5
EPS_RMS = float(np.finfo(np.float32).eps)
NAT_LOG_EXP_SET = 6


class Ctr:
    def __init__(self, es, nc, name):
        self.h = es.enter_context(nc.semaphore(name))
        self.name = name
        self.n = 0

    def inc(self, instr, k=1):
        instr.then_inc(self.h, k)
        self.n += k
        return self.n


class MK:
    def __init__(self, debug=None, phases=None):
        self.debug = debug
        self.phases = phases
        nc = bass.Bass("TRN2", target_bir_lowering=False)
        self.nc = nc
        es = contextlib.ExitStack()
        self.es = es
        nc._mk_es = es

        d = nc.dram_tensor
        self.x = d("x", [NBATCH, S, E], F32, kind="ExternalInput").ap()
        self.wq = d("wq", [E, E], F32, kind="ExternalInput").ap()
        self.wk = d("wk", [E, E], F32, kind="ExternalInput").ap()
        self.wv = d("wv", [E, E], F32, kind="ExternalInput").ap()
        self.wo = d("wo", [E, E], F32, kind="ExternalInput").ap()
        self.wfc = d("wfc", [H, E], F32, kind="ExternalInput").ap()
        self.wproj = d("wproj", [E, H], F32, kind="ExternalInput").ap()
        self.ident_in = d("ident_in", [P, P], BF16, kind="ExternalInput").ap()
        self.ones_in = d("ones_in", [P, 1], BF16, kind="ExternalInput").ap()
        self.cbias_in = d("cbias_in", [P, 2], F32, kind="ExternalInput").ap()
        self.out = d("out", [NBATCH, S, E], F32, kind="ExternalOutput").ap()

        dk = "ExternalOutput" if debug else "Internal"
        self.qsl = d("qsl", [EC * BLK], BF16, kind=dk).ap()
        self.vfs = d("vfs", [EC * BLK], BF16, kind=dk).ap()
        self.ys = d("ys", [S * E], F32, kind=dk).ap()
        self.ln2d = d("ln2d", [EC * BLK], BF16, kind=dk).ap()
        self.w1q_d = d("w1q_d", [H * E], BF16, kind=dk).ap()
        self.w1qT_d = d("w1qT_d", [HC * BLK], BF16, kind=dk).ap()
        self.w2q_d = d("w2q_d", [E * H], BF16, kind=dk).ap()
        self.w2T_d = d("w2T_d", [HC * BLK], BF16, kind=dk).ap()
        self.rowd = d("rowd", [GTOK], F32, kind="Internal").ap()
        if debug:
            self.dbg0 = d("dbg0", [P, EC * E], BF16, kind="ExternalOutput").ap()
            self.dbg2 = d("dbg2", [P, EC * E], BF16, kind="ExternalOutput").ap()
            self.dbg3 = d("dbg3", [P, E], BF16, kind="ExternalOutput").ap()
            self.dbg4 = d("dbg4", [P, E], BF16, kind="ExternalOutput").ap()

        def sb(name, shape, dt):
            return es.enter_context(nc.sbuf_tensor(name, shape, dt))

        self.big0 = sb("big0", [P, EC * E], BF16)
        self.big2 = sb("big2", [P, EC * E], BF16)
        self.ident = sb("ident", [P, P], BF16)
        self.ones = sb("ones", [P, 1], BF16)
        self.rowrec = sb("rowrec", [P, SC], F32)
        for nm in ("negm", "rstd", "ssqa", "ssqb", "t1c", "mxc", "nmx8",
                   "srow", "negm2", "rstd2", "negm3", "lam", "tA", "tB"):
            setattr(self, nm, sb(nm, [P, 2], F32))
        self.lamcol = sb("lamcol", [P, GSTR], F32)
        self.cbias = sb("cbias", [P, 2], F32)
        self.ssqrow = sb("ssqrow", [1, GTOK], F32)
        self.lamrow = self.ssqrow

        names = ["sx0", "sx1", "swq", "swk", "swv0", "swv1", "swo",
                 "sqo0", "sqo1", "sqi0", "sqi1", "svo0", "svo1", "svi",
                 "syo", "sl2o", "sl2i", "sw1a", "sw1b", "sw2a", "sw2b",
                 "sql0", "sql1", "sqs0", "sqs1", "stri", "stro", "srb",
                 "sya", "sout", "sxc", "scst", "ts", "vs", "as_", "ps",
                 "svia", "svib", "svic", "svid", "swoa", "swob", "swoc",
                 "swod"]
        self.c = {n: Ctr(es, nc, n) for n in names}
        self.ev = {}
        self.lastw = {}
        self.blk = es.enter_context(nc.Block())

        self.quant_cursor = 0
        self.quant_loads_issued = 0
        self.transp1_cursor = 0
        self.transp2_cursor = 0

    def wt(self, eng, ename, ctr, val):
        if val is None or val <= 0:
            return
        lw = self.lastw.setdefault(ename, {})
        if lw.get(ctr.name, -1) >= val:
            return
        eng.wait_ge(ctr.h, val)
        lw[ctr.name] = val

    def wte(self, eng, ename, ctr, key):
        if key in self.ev:
            self.wt(eng, ename, ctr, self.ev[key])

    def col(self, t, k):
        return t[:, k:k + 1]

    def barrier_vals(self):
        return {n: self.c[n].n for n in ("ts", "vs", "as_", "ps")}

    def ldbar(self, eng, ename):
        for n in ("ts", "vs", "as_", "ps", "sqo0", "sqo1", "svo0", "svo1",
                  "syo", "sl2o", "sqs0", "sqs1", "sout", "sya", "sxc",
                  "stri", "stro"):
            self.wt(eng, ename, self.c[n], self.c[n].n)

    def build(self):
        self.preamble()
        stop = self.phases
        for b in range(NBATCH):
            bs = f"b{b}"
            self.phase_a(b)
            if stop in ("A", bs + "A"):
                break
            self.phase_a1b(b)
            if stop in ("K", bs + "K"):
                break
            self.phase_a2(b)
            if stop in ("V", bs + "V"):
                break
            self.phase_b1(b)
            if stop in ("B1", bs + "B1"):
                break
            self.phase_b2(b)
            if stop in ("B2", bs + "B2"):
                break
            self.phase_c(b)
            if stop in ("C", bs + "C"):
                break
            self.phase_d(b)
            if stop in ("D", "Dfc", "Dg0", "Dg1", "Dg2", bs + "D"):
                break
        self.finale()
        return self.nc

    def preamble(self):
        c = self.c

        @self.blk.scalar
        def _(s):
            s.add_instruction(
                mybir.InstLoadActFuncSet(
                    name=self.nc.get_next_instruction_name(),
                    act_func_set_id=NAT_LOG_EXP_SET,
                    ins=[], outs=[],
                )
            )

        @self.blk.sync
        def _(sy):
            c["scst"].inc(sy.dma_start(out=self.ident[:], in_=self.ident_in), 16)
            c["scst"].inc(sy.dma_start(out=self.ones[:], in_=self.ones_in), 16)
            c["scst"].inc(sy.dma_start(out=self.cbias[:], in_=self.cbias_in), 16)
        self.ev[("consts",)] = c["scst"].n

        @self.blk.gpsimd
        def _(g):
            for cch in range(EC):
                ins = g.dma_start(
                    out=self.big2[:, cch * E:(cch + 1) * E],
                    in_=self.wq[cch * P:(cch + 1) * P, :])
                c["swq"].inc(ins, 16)
        self.ev[("wqld", 0)] = c["swq"].n

        self.qscope = contextlib.ExitStack()
        qsb = lambda n, s, dt: self.qscope.enter_context(
            self.nc.sbuf_tensor(n, s, dt))
        self.q_wsl = [qsb(f"q_wsl{k}", [P, E], F32) for k in range(2)]
        self.q_wb = [qsb(f"q_wb{k}", [P, E], BF16) for k in range(2)]
        self.q_quot = [qsb(f"q_quot{k}", [P, E], BF16) for k in range(2)]
        self.q_sgn = [qsb(f"q_sgn{k}", [P, E], BF16) for k in range(2)]
        self.q_sums = [qsb(f"q_sums{k}", [P, E // 64], F32) for k in range(2)]
        self.q_scl = [qsb(f"q_scl{k}", [P, E // 64], BF16) for k in range(2)]
        self.q_thr = [qsb(f"q_thr{k}", [P, E // 64], F32) for k in range(2)]
        self.trscope = contextlib.ExitStack()
        self.trbuf = self.trscope.enter_context(
            self.nc.sbuf_tensor("trbuf", [P, E], BF16))

    def quant_unit_src(self, u):
        if u < 64:
            src = bass.AP(self.wfc.tensor, u * P * E, [[E, P], [1, E]])
            dst = bass.AP(self.w1q_d.tensor, u * P * E, [[E, P], [1, E]])
        else:
            v = u - 64
            dc, qq = v // 4, v % 4
            off = dc * P * H + qq * E
            src = bass.AP(self.wproj.tensor, off, [[H, P], [1, E]])
            dst = bass.AP(self.w2q_d.tensor, off, [[H, P], [1, E]])
        return src, dst

    def quant_issue_load(self):
        u = self.quant_loads_issued
        if u >= 128:
            return
        self.quant_loads_issued += 1
        c = self.c
        src, _ = self.quant_unit_src(u)

        @self.blk.sync
        def _(sy):
            self.wte(sy, "sync", c["as_"], ("qcast", u - 2))
            c[f"sql{u % 2}"].inc(
                sy.dma_start(out=self.q_wsl[u % 2][:], in_=src), 16)
        self.ev[("qld", u)] = c[f"sql{u % 2}"].n

    def quant_step(self, n):
        for _ in range(n):
            u = self.quant_cursor
            if u >= 128:
                return
            self.quant_cursor += 1
            while self.quant_loads_issued < min(u + 2, 128):
                self.quant_issue_load()
            c = self.c
            k = u % 2
            wsl = self.q_wsl[k]
            wb = self.q_wb[k]
            absq = self.q_quot[k]
            sgn = self.q_sgn[k]
            sums, scl, thr = self.q_sums[k], self.q_scl[k], self.q_thr[k]

            @self.blk.scalar
            def _(s):
                self.wte(s, "scalar", c[f"sql{k}"], ("qld", u))
                self.wte(s, "scalar", c["vs"], ("qw", u - 2))
                self.wte(s, "scalar", c[f"sqs{k}"], ("qst", u - 2))
                c["as_"].inc(s.copy(wb[:], wsl[:]))
                self.ev[("qcast", u)] = c["as_"].n
                self.wt(s, "scalar", c["as_"], c["as_"].n)
                c["as_"].inc(s.activation(out=absq[:], in_=wb[:], func=AF.Abs))
                self.wt(s, "scalar", c["as_"], c["as_"].n)
                c["as_"].inc(s.activation(out=sgn[:], in_=wb[:], func=AF.Sign))
            self.ev[("qact", u)] = c["as_"].n

            @self.blk.vector
            def _(v):
                sv = lambda: self.wt(v, "vector", c["vs"], c["vs"].n)
                self.wte(v, "vector", c["as_"], ("qcast", u))
                c["vs"].inc(v.tensor_reduce(
                    sums[:], wb[:].rearrange("p (g k) -> p g k", k=64),
                    axis=AX, op=ALU.add, apply_absolute_value=True))
                sv()
                c["vs"].inc(v.tensor_scalar(
                    scl[:], sums[:], 1.0 / 64, 1e-8,
                    op0=ALU.mult, op1=ALU.max))
                sv()
                c["vs"].inc(v.tensor_scalar_mul(thr[:], scl[:], 0.5))
                sv()
                self.wte(v, "vector", c["as_"], ("qact", u))
                thr_bc = bass.AP(thr, 0, [[E // 64, P], [1, E // 64], [0, 64]])
                c["vs"].inc(v.tensor_tensor(
                    absq[:].rearrange("p (g k) -> p g k", k=64),
                    absq[:].rearrange("p (g k) -> p g k", k=64),
                    thr_bc, op=ALU.is_gt))
                sv()
                scl_bc = bass.AP(scl, 0, [[E // 64, P], [1, E // 64], [0, 64]])
                c["vs"].inc(v.tensor_tensor(
                    sgn[:].rearrange("p (g k) -> p g k", k=64),
                    sgn[:].rearrange("p (g k) -> p g k", k=64),
                    scl_bc, op=ALU.mult))
                sv()
                c["vs"].inc(v.tensor_mul(wb[:], absq[:], sgn[:]))
            self.ev[("qw", u)] = c["vs"].n

            for us in ([u - 1] if u >= 1 else []) + ([u] if u == 127 else []):
                ks = us % 2

                @self.blk.gpsimd
                def _(g):
                    self.wte(g, "gpsimd", c["vs"], ("qw", us))
                    _, dst = self.quant_unit_src(us)
                    c[f"sqs{ks}"].inc(g.dma_start(
                        out=dst, in_=self.q_wb[ks][:]), 16)
                self.ev[("qst", us)] = c[f"sqs{ks}"].n
                if us == 63:
                    self.ev[("qfc_done",)] = (c["sqs0"].n, c["sqs1"].n)
                if us == 127:
                    self.ev[("qpr_done",)] = (c["sqs0"].n, c["sqs1"].n)
            while self.quant_loads_issued < min(u + 3, 128):
                self.quant_issue_load()

    def transp1_step(self, n):
        if ("qfc_done",) not in self.ev:
            return
        for _ in range(n):
            u = self.transp1_cursor
            if u >= EC * 4:
                return
            self.transp1_cursor += 1
            dc, q = u // 4, u % 4
            c = self.c

            @self.blk.sync
            def _(sy):
                v0, v1 = self.ev[("qfc_done",)]
                self.wt(sy, "sync", c["sqs0"], v0)
                self.wt(sy, "sync", c["sqs1"], v1)
                self.wte(sy, "sync", c["stro"], ("t1o", u - 1))
                src = bass.AP(self.w1q_d.tensor, q * 2048 * E + dc * P,
                              [[E, 2048], [1, P]])
                c["stri"].inc(
                    sy.dma_start(out=self.trbuf[:], in_=src, transpose=True),
                    16)
            self.ev[("t1i", u)] = c["stri"].n

            @self.blk.gpsimd
            def _(g):
                self.wte(g, "gpsimd", c["stri"], ("t1i", u))
                dst = bass.AP(
                    self.w1qT_d.tensor,
                    q * 16 * BLK + dc * P,
                    [[E, P], [BLK, 16], [1, P]])
                c["stro"].inc(g.dma_start(
                    out=dst,
                    in_=self.trbuf[:].rearrange("p (j i) -> p j i", i=P)), 16)
            self.ev[("t1o", u)] = c["stro"].n
            if u == EC * 4 - 1:
                self.ev[("w1T_done",)] = c["stro"].n

    def transp2_step(self, n):
        if ("qpr_done",) not in self.ev:
            return
        for _ in range(n):
            jc = self.transp2_cursor
            if jc >= HC:
                return
            self.transp2_cursor += 1
            c = self.c

            @self.blk.sync
            def _(sy):
                v0, v1 = self.ev[("qpr_done",)]
                self.wt(sy, "sync", c["sqs0"], v0)
                self.wt(sy, "sync", c["sqs1"], v1)
                self.wte(sy, "sync", c["stro"], ("t2o", jc - 1))
                if jc == 0:
                    self.wte(sy, "sync", c["stro"], ("w1T_done",))
                src = bass.AP(self.w2q_d.tensor, jc * P, [[H, E], [1, P]])
                c["stri"].inc(sy.dma_start(
                    out=self.trbuf[:, 0:E], in_=src, transpose=True), 16)
            self.ev[("t2i", jc)] = c["stri"].n

            @self.blk.gpsimd
            def _(g):
                self.wte(g, "gpsimd", c["stri"], ("t2i", jc))
                dst = bass.AP(self.w2T_d.tensor, jc * BLK, [[E, P], [1, E]])
                c["stro"].inc(g.dma_start(out=dst, in_=self.trbuf[:, 0:E]),
                              16)
            self.ev[("t2o", jc)] = c["stro"].n
            if jc == HC - 1:
                self.ev[("w2T_done",)] = c["stro"].n

    def phase_a(self, b):
        c, nc = self.c, self.nc
        sc = contextlib.ExitStack()
        ssb = lambda n, s, dt: sc.enter_context(nc.sbuf_tensor(n, s, dt))
        sps = lambda n, s, dt: sc.enter_context(nc.psum_tensor(n, s, dt))
        xst = [ssb(f"a{b}_xst{k}", [P, E], F32) for k in range(2)]
        ln1s = [ssb(f"a{b}_ln1s{k}", [P, E], BF16) for k in range(2)]
        qstage = [ssb(f"a{b}_qst{k}", [P, E], BF16) for k in range(2)]
        psQ = sps(f"a{b}_psQ", [P, E], F32)
        psT = sps(f"a{b}_psT", [P, E], BF16)
        bar = self.barrier_vals()

        @self.blk.sync
        def _(sy):
            for n, v in bar.items():
                self.wt(sy, "sync", c[n], v)
            self.wte(sy, "sync", c["sout"], ("outdone", b - 1))
            for i in (0, 1):
                ins = sy.dma_start(
                    out=xst[i % 2][:], in_=self.x[b, i * P:(i + 1) * P, :])
                c[f"sx{i % 2}"].inc(ins, 16)
                self.ev[("xld", b, i)] = c[f"sx{i % 2}"].n

        for i in range(SC + 1):
            if i < SC:
                @self.blk.vector
                def _(v):
                    self.wte(v, "vector", c[f"sx{i % 2}"], ("xld", b, i))
                    self.wte(v, "vector", c["as_"], ("sq", b, i - 2))
                    c["vs"].inc(v.reduce_sum(
                        self.col(self.tA, i % 2), xst[i % 2][:], axis=AX))
                    self.wt(v, "vector", c["vs"], c["vs"].n)
                    c["vs"].inc(v.tensor_scalar_mul(
                        self.col(self.negm, i % 2),
                        self.col(self.tA, i % 2), -1.0 / E))
                self.ev[("negm", b, i)] = c["vs"].n

            @self.blk.scalar
            def _(s):
                if i - 1 >= 0:
                    self.wte(s, "scalar", c["ts"], ("T", b, i - 1))
                    dst = bass.AP(self.big0, (i - 1) * P,
                                  [[EC * E, P], [E, EC], [1, P]])
                    c["as_"].inc(s.copy(
                        dst, psT[:].rearrange("p (t r) -> p t r", r=P)))
                    self.ev[("evT", b, i - 1)] = c["as_"].n
                if i < SC:
                    self.wte(s, "scalar", c["scst"], ("consts",))
                    self.wte(s, "scalar", c["vs"], ("negm", b, i))
                    self.wte(s, "scalar", c["ts"], ("T", b, i - 2))
                    c["as_"].inc(s.activation(
                        out=ln1s[i % 2][:], in_=xst[i % 2][:], func=AF.Square,
                        bias=self.col(self.negm, i % 2),
                        accum_out=self.col(self.ssqa, i % 2)))
                    self.ev[("sq", b, i)] = c["as_"].n
                    self.wt(s, "scalar", c["as_"], c["as_"].n)
                    c["as_"].inc(s.activation(
                        out=self.col(self.t1c, i % 2),
                        in_=self.col(self.ssqa, i % 2), func=AF.Ln,
                        scale=1.0 / E, bias=self.col(self.cbias, 0)))
                    self.wt(s, "scalar", c["as_"], c["as_"].n)
                    c["as_"].inc(s.activation(
                        out=self.col(self.rstd, i % 2),
                        in_=self.col(self.t1c, i % 2), func=AF.Exp,
                        scale=-0.5))
                    self.ev[("rstd", b, i)] = c["as_"].n

            if i < SC:
                @self.blk.vector
                def _(v):
                    self.wte(v, "vector", c["as_"], ("rstd", b, i))
                    c["vs"].inc(v.tensor_scalar(
                        ln1s[i % 2][:], xst[i % 2][:],
                        self.col(self.negm, i % 2),
                        self.col(self.rstd, i % 2),
                        op0=ALU.add, op1=ALU.mult))
                self.ev[("ln1s", b, i)] = c["vs"].n

            @self.blk.tensor
            def _(t):
                if i - 1 >= 0:
                    j = i - 1
                    self.wte(t, "tensor", c["as_"], ("evT", b, j))
                    self.wte(t, "tensor", c["vs"], ("evQ", b, j - 1))
                    self.wte(t, "tensor", c["swq"], ("wqld", b))
                    for e in range(EC):
                        for sbk in range(4):
                            mm = t.matmul(
                                psQ[:, sbk * NB:(sbk + 1) * NB],
                                self.big0[:, e * E + j * P:
                                          e * E + j * P + P],
                                self.big2[:, e * E + sbk * NB:
                                          e * E + (sbk + 1) * NB],
                                start=(e == 0), stop=(e == EC - 1))
                    c["ts"].inc(mm)
                    self.ev[("Q", b, j)] = c["ts"].n
                if i < SC:
                    self.wte(t, "tensor", c["scst"], ("consts",))
                    self.wte(t, "tensor", c["vs"], ("ln1s", b, i))
                    self.wte(t, "tensor", c["as_"], ("evT", b, i - 1))
                    for tt in range(EC):
                        mm = t.transpose(
                            psT[:, tt * P:(tt + 1) * P],
                            ln1s[i % 2][:, tt * P:(tt + 1) * P],
                            self.ident[:])
                    c["ts"].inc(mm)
                    self.ev[("T", b, i)] = c["ts"].n

            if i - 1 >= 0:
                j = i - 1

                @self.blk.vector
                def _(v):
                    self.wte(v, "vector", c["ts"], ("Q", b, j))
                    self.wte(v, "vector", c[f"sqo{j % 2}"], ("qst", b, j - 2))
                    c["vs"].inc(v.tensor_copy(qstage[j % 2][:], psQ[:]))
                self.ev[("evQ", b, j)] = c["vs"].n

                @self.blk.gpsimd
                def _(g):
                    self.wte(g, "gpsimd", c["vs"], ("evQ", b, j))
                    dst = bass.AP(self.qsl.tensor, j * P,
                                  [[E, P], [BLK, EC], [1, P]])
                    c[f"sqo{j % 2}"].inc(g.dma_start(
                        out=dst,
                        in_=qstage[j % 2][:].rearrange(
                            "p (a i) -> p a i", i=P)), 16)
                self.ev[("qst", b, j)] = c[f"sqo{j % 2}"].n

            if i + 2 < SC:
                @self.blk.sync
                def _(sy):
                    self.wte(sy, "sync", c["as_"], ("sq", b, i))
                    self.wte(sy, "sync", c["vs"], ("ln1s", b, i))
                    c[f"sx{i % 2}"].inc(sy.dma_start(
                        out=xst[i % 2][:],
                        in_=self.x[b, (i + 2) * P:(i + 3) * P, :]), 16)
                self.ev[("xld", b, i + 2)] = c[f"sx{i % 2}"].n

            if b == 0:
                self.quant_step(1)
        sc.close()

    def phase_a1b(self, b):
        c, nc = self.c, self.nc
        sc = contextlib.ExitStack()
        wkh = sc.enter_context(
            nc.sbuf_tensor(f"k{b}_wkh", [P, EC * 1024], BF16))
        psK = [sc.enter_context(
            nc.psum_tensor(f"k{b}_psK{k}", [P, 1024], F32)) for k in range(2)]

        for h in range(2):
            @self.blk.gpsimd
            def _(g):
                if h == 0:
                    self.ldbar(g, "gpsimd")
                if h == 1:
                    self.wte(g, "gpsimd", c["ts"], ("K", b, 0, SC - 1))
                for cc in range(EC):
                    src = bass.AP(self.wk.tensor, cc * P * E + h * 1024,
                                  [[E, P], [1, 1024]])
                    c["swk"].inc(g.dma_start(
                        out=wkh[:, cc * 1024:(cc + 1) * 1024], in_=src), 16)
                    self.ev[("wkld", b, h, cc)] = c["swk"].n

            for i in range(SC):
                @self.blk.tensor
                def _(t):
                    self.wte(t, "tensor", c["swk"], ("wkld", b, h, EC - 1))
                    if i >= 2:
                        self.wte(t, "tensor", c["vs"], ("evK", b, h, i - 2))
                    elif h == 1:
                        self.wte(t, "tensor", c["vs"],
                                 ("evK", b, 0, SC + i - 2))
                    if h == 0 and i == 0:
                        self.wte(t, "tensor", c["as_"], ("evT", b, SC - 1))
                        self.wte(t, "tensor", c["ts"], ("Q", b, SC - 1))
                        self.wte(t, "tensor", c["vs"], ("evQ", b, SC - 1))
                    for cc in range(EC):
                        for sbk in range(2):
                            mm = t.matmul(
                                psK[i % 2][:, sbk * NB:(sbk + 1) * NB],
                                self.big0[:, cc * E + i * P:
                                          cc * E + i * P + P],
                                wkh[:, cc * 1024 + sbk * NB:
                                    cc * 1024 + (sbk + 1) * NB],
                                start=(cc == 0), stop=(cc == EC - 1))
                    c["ts"].inc(mm)
                    self.ev[("K", b, h, i)] = c["ts"].n

                @self.blk.vector
                def _(v):
                    self.wte(v, "vector", c["ts"], ("K", b, h, i))
                    c["vs"].inc(v.tensor_copy(
                        self.big2[:, i * E + h * 1024:
                                  i * E + h * 1024 + 1024],
                        psK[i % 2][:]))
                self.ev[("evK", b, h, i)] = c["vs"].n

                if b == 0:
                    self.quant_step(1)
        sc.close()

    def phase_a2(self, b):
        c, nc = self.c, self.nc
        sc = contextlib.ExitStack()
        wvb = [sc.enter_context(
            nc.sbuf_tensor(f"v{b}_wvb{k}", [P, E], BF16)) for k in range(2)]
        vstage = [sc.enter_context(
            nc.sbuf_tensor(f"v{b}_vst{k}", [P, E], BF16)) for k in range(2)]
        psV = [sc.enter_context(
            nc.psum_tensor(f"v{b}_psV{k}", [P, E], F32)) for k in range(2)]

        def wv_load(fc):
            @self.blk.gpsimd
            def _(g):
                if fc < 2:
                    self.ldbar(g, "gpsimd")
                self.wte(g, "gpsimd", c["ts"], ("V", b, fc - 2))
                src = bass.AP(self.wv.tensor, fc * P,
                              [[E, P], [P * E, EC], [1, P]])
                c[f"swv{fc % 2}"].inc(g.dma_start(out=wvb[fc % 2][:],
                                                  in_=src), 16)
            self.ev[("wvld", b, fc)] = c[f"swv{fc % 2}"].n

        wv_load(0)
        wv_load(1)
        for fc in range(EC):
            @self.blk.tensor
            def _(t):
                self.wte(t, "tensor", c[f"swv{fc % 2}"], ("wvld", b, fc))
                self.wte(t, "tensor", c["vs"], ("evV", b, fc - 2))
                if fc == 0:
                    self.wte(t, "tensor", c["vs"], ("evK", b, 1, SC - 1))
                for tt in range(EC):
                    for sbk in range(4):
                        mm = t.matmul(
                            psV[fc % 2][:, sbk * NB:(sbk + 1) * NB],
                            wvb[fc % 2][:, tt * P:(tt + 1) * P],
                            self.big0[:, tt * E + sbk * NB:
                                      tt * E + (sbk + 1) * NB],
                            start=(tt == 0), stop=(tt == EC - 1))
                c["ts"].inc(mm)
                self.ev[("V", b, fc)] = c["ts"].n

            @self.blk.vector
            def _(v):
                self.wte(v, "vector", c["ts"], ("V", b, fc))
                self.wte(v, "vector", c[f"svo{fc % 2}"], ("vst", b, fc - 2))
                c["vs"].inc(v.tensor_copy(vstage[fc % 2][:], psV[fc % 2][:]))
            self.ev[("evV", b, fc)] = c["vs"].n

            @self.blk.gpsimd
            def _(g):
                self.wte(g, "gpsimd", c["vs"], ("evV", b, fc))
                dst = bass.AP(self.vfs.tensor, fc * BLK, [[E, P], [1, E]])
                c[f"svo{fc % 2}"].inc(g.dma_start(out=dst,
                                                  in_=vstage[fc % 2][:]), 16)
            self.ev[("vst", b, fc)] = c[f"svo{fc % 2}"].n

            if fc + 2 < EC:
                wv_load(fc + 2)
            if b == 0:
                self.quant_step(3)
                self.transp1_step(4)
        if b == 0:
            self.quant_step(128)
            self.transp1_step(EC * 4)
        sc.close()

    def phase_b1(self, b):
        c, nc = self.c, self.nc
        sc = contextlib.ExitStack()
        qin = [sc.enter_context(
            nc.sbuf_tensor(f"b1{b}_qin{k}", [P, E], BF16)) for k in range(2)]
        psb = [sc.enter_context(
            nc.sbuf_tensor(f"b1{b}_psb{k}", [P, E], BF16)) for k in range(2)]
        psS = sc.enter_context(nc.psum_tensor(f"b1{b}_psS", [P, E], F32))
        psTb = sc.enter_context(nc.psum_tensor(f"b1{b}_psTb", [P, E], BF16))

        def q_load(j):
            @self.blk.sync
            def _(sy):
                if j < 2:
                    self.ldbar(sy, "sync")
                self.wt(sy, "sync", c["sqo0"], c["sqo0"].n)
                self.wt(sy, "sync", c["sqo1"], c["sqo1"].n)
                self.wte(sy, "sync", c["ts"], ("S", b, j - 2))
                src = bass.AP(self.qsl.tensor, j * BLK, [[E, P], [1, E]])
                c[f"sqi{j % 2}"].inc(sy.dma_start(out=qin[j % 2][:],
                                                  in_=src), 16)
            self.ev[("qld2", b, j)] = c[f"sqi{j % 2}"].n

        q_load(0)
        q_load(1)
        for j in range(SC + 1):
            @self.blk.tensor
            def _(t):
                if j < SC:
                    self.wte(t, "tensor", c[f"sqi{j % 2}"], ("qld2", b, j))
                    self.wte(t, "tensor", c["as_"], ("Exp", b, j - 1))
                    if j == 0:
                        self.wte(t, "tensor", c["vs"],
                                 ("evK", b, 1, SC - 1))
                        self.wte(t, "tensor", c["vs"], ("evV", b, EC - 1))
                    for cc in range(SC):
                        for sbk in range(4):
                            mm = t.matmul(
                                psS[:, sbk * NB:(sbk + 1) * NB],
                                qin[j % 2][:, cc * P:(cc + 1) * P],
                                self.big2[:, cc * E + sbk * NB:
                                          cc * E + (sbk + 1) * NB],
                                start=(cc == 0), stop=(cc == SC - 1))
                    c["ts"].inc(mm)
                    self.ev[("S", b, j)] = c["ts"].n
                if j - 1 >= 0:
                    jj = j - 1
                    self.wte(t, "tensor", c["as_"], ("Exp", b, jj))
                    self.wte(t, "tensor", c["as_"], ("evTb", b, jj - 1))
                    for tt in range(EC):
                        mm = t.transpose(
                            psTb[:, tt * P:(tt + 1) * P],
                            psb[jj % 2][:, tt * P:(tt + 1) * P],
                            self.ident[:])
                    c["ts"].inc(mm)
                    self.ev[("Tb", b, jj)] = c["ts"].n

            if j < SC:
                @self.blk.vector
                def _(v):
                    self.wte(v, "vector", c["ts"], ("S", b, j))
                    self.wte(v, "vector", c["as_"], ("Exp", b, j - 2))
                    c["vs"].inc(v.reduce_max(
                        self.col(self.mxc, j % 2), psS[:], axis=AX))
                    self.wt(v, "vector", c["vs"], c["vs"].n)
                    c["vs"].inc(v.tensor_scalar_mul(
                        self.col(self.nmx8, j % 2),
                        self.col(self.mxc, j % 2), -0.125))
                self.ev[("nmx8", b, j)] = c["vs"].n

                @self.blk.scalar
                def _(s):
                    self.wte(s, "scalar", c["vs"], ("nmx8", b, j))
                    self.wte(s, "scalar", c["ts"], ("Tb", b, j - 2))
                    c["as_"].inc(s.activation(
                        out=psb[j % 2][:], in_=psS[:], func=AF.Exp,
                        scale=0.125, bias=self.col(self.nmx8, j % 2),
                        accum_out=self.col(self.srow, j % 2)))
                self.ev[("Exp", b, j)] = c["as_"].n

                @self.blk.vector
                def _(v):
                    self.wte(v, "vector", c["as_"], ("Exp", b, j))
                    c["vs"].inc(v.reciprocal(
                        self.rowrec[:, j:j + 1], self.col(self.srow, j % 2)))
                self.ev[("rr", b, j)] = c["vs"].n

                if self.debug and b == 1 and j == 0:
                    @self.blk.gpsimd
                    def _(g):
                        self.wte(g, "gpsimd", c["as_"], ("Exp", b, j))
                        c["sout"].inc(g.dma_start(
                            out=self.dbg3, in_=psb[0][:]), 16)
                        self.wte(g, "gpsimd", c[f"sqi{0}"], ("qld2", b, 0))
                        c["sout"].inc(g.dma_start(
                            out=self.dbg4, in_=qin[0][:]), 16)

            if j - 1 >= 0:
                jj = j - 1

                @self.blk.scalar
                def _(s):
                    self.wte(s, "scalar", c["ts"], ("Tb", b, jj))
                    c["as_"].inc(s.copy(
                        self.big0[:, jj * E:(jj + 1) * E], psTb[:]))
                self.ev[("evTb", b, jj)] = c["as_"].n

            if j + 2 < SC:
                q_load(j + 2)
            if b == 0:
                self.quant_step(2)
                self.transp2_step(2)
        sc.close()

    def phase_b2(self, b):
        c, nc = self.c, self.nc
        sc = contextlib.ExitStack()
        psO = [sc.enter_context(
            nc.psum_tensor(f"b2{b}_psO{k}", [P, E], F32)) for k in range(2)]

        @self.blk.sync
        def _(sy):
            self.wte(sy, "sync", c["ts"], ("S", b, SC - 1))
            self.wt(sy, "sync", c["svo0"], c["svo0"].n)
            self.wt(sy, "sync", c["svo1"], c["svo1"].n)
            for cc in range(EC):
                sv = c[("svia", "svib", "svic", "svid")[cc // 4]]
                src = bass.AP(self.vfs.tensor, cc * BLK, [[E, P], [1, E]])
                sv.inc(sy.dma_start(
                    out=self.big2[:, cc * E:(cc + 1) * E], in_=src), 16)
                self.ev[("vfg", b, cc // 4)] = sv.n

        for j in range(SC):
            @self.blk.tensor
            def _(t):
                self.wte(t, "tensor", c["vs"], ("evO", b, j - 2))
                if j == 0:
                    self.wte(t, "tensor", c["as_"], ("evTb", b, SC - 1))
                    self.wte(t, "tensor", c["vs"], ("rr", b, SC - 1))
                for tt in range(EC):
                    self.wte(t, "tensor",
                             c[("svia", "svib", "svic", "svid")[tt // 4]],
                             ("vfg", b, tt // 4))
                    for sbk in range(4):
                        mm = t.matmul(
                            psO[j % 2][:, sbk * NB:(sbk + 1) * NB],
                            self.big0[:, j * E + tt * P: j * E + tt * P + P],
                            self.big2[:, tt * E + sbk * NB:
                                      tt * E + (sbk + 1) * NB],
                            start=(tt == 0), stop=(tt == EC - 1))
                c["ts"].inc(mm)
                self.ev[("O", b, j)] = c["ts"].n

            @self.blk.vector
            def _(v):
                self.wte(v, "vector", c["ts"], ("O", b, j))
                c["vs"].inc(v.tensor_single_scalar(
                    self.big0[:, j * E:(j + 1) * E], psO[j % 2][:],
                    self.rowrec[:, j:j + 1], op=ALU.mult))
            self.ev[("evO", b, j)] = c["vs"].n

            if b == 0:
                self.quant_step(2)
                self.transp2_step(2)
        if b == 0:
            self.quant_step(128)
            self.transp2_step(HC)

        @self.blk.gpsimd
        def _(g):
            self.wte(g, "gpsimd", c["ts"], ("O", b, SC - 1))
            if b == 0:
                self.wte(g, "gpsimd", c["stro"], ("w2T_done",))
            for cc in range(EC):
                sw = c[("swoa", "swob", "swoc", "swod")[cc // 4]]
                ins = g.dma_start(
                    out=self.big2[:, cc * E:(cc + 1) * E],
                    in_=self.wo[cc * P:(cc + 1) * P, :])
                sw.inc(ins, 16)
                self.ev[("wog", b, cc // 4)] = sw.n
        sc.close()
        if b == 0:
            self.trscope.close()
            self.qscope.close()

    def phase_c(self, b):
        c, nc = self.c, self.nc
        sc = contextlib.ExitStack()
        ystage = sc.enter_context(nc.sbuf_tensor(f"c{b}_yst", [P, E], F32))
        ln2s = [sc.enter_context(
            nc.sbuf_tensor(f"c{b}_ln2s{k}", [P, E], BF16)) for k in range(2)]
        l2stage = sc.enter_context(nc.sbuf_tensor(f"c{b}_l2st", [P, E], BF16))
        psC = sc.enter_context(nc.psum_tensor(f"c{b}_psC", [P, E], F32))
        psTc = sc.enter_context(nc.psum_tensor(f"c{b}_psTc", [P, E], BF16))

        for i in range(SC + 1):
            @self.blk.tensor
            def _(t):
                if i < SC:
                    self.wte(t, "tensor", c["vs"], ("ap1", b, i - 1))
                    if i == 0:
                        self.wte(t, "tensor", c["vs"], ("evO", b, SC - 1))
                    for tt in range(EC):
                        self.wte(t, "tensor",
                                 c[("swoa", "swob", "swoc", "swod")[tt // 4]],
                                 ("wog", b, tt // 4))
                        for sbk in range(4):
                            mm = t.matmul(
                                psC[:, sbk * NB:(sbk + 1) * NB],
                                self.big0[:, tt * E + i * P:
                                          tt * E + i * P + P],
                                self.big2[:, tt * E + sbk * NB:
                                          tt * E + (sbk + 1) * NB],
                                start=(tt == 0), stop=(tt == EC - 1))
                    c["ts"].inc(mm)
                    self.ev[("PRJ", b, i)] = c["ts"].n
                if i - 1 >= 0:
                    ii = i - 1
                    self.wte(t, "tensor", c["vs"], ("ln2s", b, ii))
                    self.wte(t, "tensor", c["as_"], ("evTc", b, ii - 1))
                    for tt in range(EC):
                        mm = t.transpose(
                            psTc[:, tt * P:(tt + 1) * P],
                            ln2s[ii % 2][:, tt * P:(tt + 1) * P],
                            self.ident[:])
                    c["ts"].inc(mm)
                    self.ev[("Tc", b, ii)] = c["ts"].n

            if i < SC:
                @self.blk.vector
                def _(v):
                    self.wte(v, "vector", c["ts"], ("PRJ", b, i))
                    self.wte(v, "vector", c["as_"], ("sqC", b, i - 2))
                    c["vs"].inc(v.reduce_sum(
                        self.col(self.tA, i % 2), psC[:], axis=AX))
                    self.wt(v, "vector", c["vs"], c["vs"].n)
                    c["vs"].inc(v.tensor_scalar_mul(
                        self.col(self.negm, i % 2),
                        self.col(self.tA, i % 2), -1.0 / E))
                self.ev[("negmC", b, i)] = c["vs"].n

                @self.blk.scalar
                def _(s):
                    self.wte(s, "scalar", c["vs"], ("negmC", b, i))
                    self.wte(s, "scalar", c["ts"], ("Tc", b, i - 2))
                    c["as_"].inc(s.activation(
                        out=ln2s[i % 2][:], in_=psC[:], func=AF.Square,
                        bias=self.col(self.negm, i % 2),
                        accum_out=self.col(self.ssqa, i % 2)))
                    self.ev[("sqC", b, i)] = c["as_"].n
                    self.wt(s, "scalar", c["as_"], c["as_"].n)
                    c["as_"].inc(s.activation(
                        out=self.col(self.t1c, i % 2),
                        in_=self.col(self.ssqa, i % 2), func=AF.Ln,
                        scale=1.0 / E, bias=self.col(self.cbias, 0)))
                    self.wt(s, "scalar", c["as_"], c["as_"].n)
                    c["as_"].inc(s.activation(
                        out=self.col(self.rstd, i % 2),
                        in_=self.col(self.t1c, i % 2), func=AF.Exp,
                        scale=-0.5))
                    self.ev[("rstdC", b, i)] = c["as_"].n

                @self.blk.vector
                def _(v):
                    self.wte(v, "vector", c["as_"], ("rstdC", b, i))
                    self.wte(v, "vector", c["syo"], ("yst", b, i - 1))
                    c["vs"].inc(v.tensor_scalar(
                        ystage[:], psC[:],
                        self.col(self.negm, i % 2),
                        self.col(self.rstd, i % 2),
                        op0=ALU.add, op1=ALU.mult))
                self.ev[("ap1", b, i)] = c["vs"].n

                @self.blk.gpsimd
                def _(g):
                    self.wte(g, "gpsimd", c["vs"], ("ap1", b, i))
                    c["sxc"].inc(g.dma_start(
                        out=ystage[:], in_=self.x[b, i * P:(i + 1) * P, :],
                        accum_op=ALU.add), 16)
                self.ev[("xacc", b, i)] = c["sxc"].n

                @self.blk.vector
                def _(v):
                    self.wte(v, "vector", c["sxc"], ("xacc", b, i))
                    c["vs"].inc(v.reduce_sum(
                        self.col(self.tB, i % 2), ystage[:], axis=AX))
                    self.wt(v, "vector", c["vs"], c["vs"].n)
                    c["vs"].inc(v.tensor_scalar_mul(
                        self.col(self.negm2, i % 2),
                        self.col(self.tB, i % 2), -1.0 / E))
                self.ev[("negm2", b, i)] = c["vs"].n

                @self.blk.gpsimd
                def _(g):
                    self.wte(g, "gpsimd", c["sxc"], ("xacc", b, i))
                    dst = bass.AP(self.ys.tensor, i * BLK, [[E, P], [1, E]])
                    c["syo"].inc(g.dma_start(out=dst, in_=ystage[:]), 16)
                self.ev[("yst", b, i)] = c["syo"].n

                @self.blk.scalar
                def _(s):
                    self.wte(s, "scalar", c["vs"], ("negm2", b, i))
                    c["as_"].inc(s.activation(
                        out=ln2s[i % 2][:], in_=ystage[:], func=AF.Square,
                        bias=self.col(self.negm2, i % 2),
                        accum_out=self.col(self.ssqb, i % 2)))
                    self.wt(s, "scalar", c["as_"], c["as_"].n)
                    c["as_"].inc(s.activation(
                        out=self.col(self.t1c, i % 2),
                        in_=self.col(self.ssqb, i % 2), func=AF.Ln,
                        scale=1.0 / E, bias=self.col(self.cbias, 0)))
                    self.wt(s, "scalar", c["as_"], c["as_"].n)
                    c["as_"].inc(s.activation(
                        out=self.col(self.rstd2, i % 2),
                        in_=self.col(self.t1c, i % 2), func=AF.Exp,
                        scale=-0.5))
                    self.ev[("rstd2", b, i)] = c["as_"].n

                @self.blk.vector
                def _(v):
                    self.wte(v, "vector", c["as_"], ("rstd2", b, i))
                    c["vs"].inc(v.tensor_scalar(
                        ln2s[i % 2][:], ystage[:],
                        self.col(self.negm2, i % 2),
                        self.col(self.rstd2, i % 2),
                        op0=ALU.add, op1=ALU.mult))
                self.ev[("ln2s", b, i)] = c["vs"].n

            if i - 1 >= 0:
                ii = i - 1

                @self.blk.scalar
                def _(s):
                    self.wte(s, "scalar", c["ts"], ("Tc", b, ii))
                    self.wte(s, "scalar", c["sl2o"], ("l2st", b, ii - 1))
                    c["as_"].inc(s.copy(l2stage[:], psTc[:]))
                self.ev[("evTc", b, ii)] = c["as_"].n

                @self.blk.gpsimd
                def _(g):
                    self.wte(g, "gpsimd", c["as_"], ("evTc", b, ii))
                    dst = bass.AP(self.ln2d.tensor, ii * P,
                                  [[E, P], [BLK, EC], [1, P]])
                    c["sl2o"].inc(g.dma_start(
                        out=dst,
                        in_=l2stage[:].rearrange("p (t r) -> p t r", r=P)),
                        16)
                self.ev[("l2st", b, ii)] = c["sl2o"].n
        sc.close()

    def phase_d(self, b):
        c, nc = self.c, self.nc
        sc = contextlib.ExitStack()
        ssb = lambda n, s, dt: sc.enter_context(nc.sbuf_tensor(n, s, dt))
        ln2fg = [ssb(f"d{b}_l2g{k}", [P, EC * GTOK], BF16) for k in range(2)]
        wtile = [ssb(f"d{b}_wt{k}", [P, E], BF16) for k in range(2)]
        w2sl = [ssb(f"d{b}_w2{k}", [P, 1024], BF16) for k in range(2)]
        projsb = [ssb(f"d{b}_pj{k}", [P, E], BF16) for k in range(GSTR)]
        rstage = [ssb(f"d{b}_rs{k}", [P, NB], F32) for k in range(2)]
        g2st = [ssb(f"d{b}_g2{k}", [P, NB], BF16) for k in range(2)]
        outstage = ssb(f"d{b}_outst", [P, E], F32)

        def l2g_load(g_):
            @self.blk.sync
            def _(sy):
                if g_ == 0:
                    self.ldbar(sy, "sync")
                self.wte(sy, "sync", c["sl2o"], ("l2st", b, SC - 1))
                self.wte(sy, "sync", c["ts"], ("fc", b, g_ - 2, HC - 1))
                for tt in range(EC):
                    src = bass.AP(self.ln2d.tensor, tt * BLK + g_ * GTOK,
                                  [[E, P], [1, GTOK]])
                    c["sl2i"].inc(sy.dma_start(
                        out=ln2fg[g_ % 2][:, tt * GTOK:(tt + 1) * GTOK],
                        in_=src), 16)
            self.ev[("l2ld", b, g_)] = c["sl2i"].n

        def w1_load(g_, jc):
            @self.blk.sync
            def _(sy):
                if g_ == 0 and jc < 2:
                    self.ldbar(sy, "sync")
                self.wte(sy, "sync", c["stro"], ("w1T_done",))
                if jc >= 2:
                    self.wte(sy, "sync", c["ts"], ("fc", b, g_, jc - 2))
                else:
                    self.wte(sy, "sync", c["ts"],
                             ("fc", b, g_ - 1, HC + jc - 2))
                src = bass.AP(self.w1qT_d.tensor, jc * BLK, [[E, P], [1, E]])
                sw = c["sw1a"] if jc % 2 == 0 else c["sw1b"]
                sw.inc(sy.dma_start(out=wtile[jc % 2][:], in_=src), 16)
            sw = c["sw1a"] if jc % 2 == 0 else c["sw1b"]
            self.ev[("w1ld", b, g_, jc)] = sw.n

        l2g_load(0)
        for g_ in range(NGRP):
            scF = contextlib.ExitStack()
            psF = [scF.enter_context(
                nc.psum_tensor(f"d{b}_{g_}_psF{k}", [P, NB], F32))
                for k in range(2)]
            psR = scF.enter_context(
                nc.psum_tensor(f"d{b}_{g_}_psR", [P, NB], F32))

            w1_load(g_, 0)
            w1_load(g_, 1)
            for jc in range(HC):
                @self.blk.tensor
                def _(t):
                    sw = c["sw1a"] if jc % 2 == 0 else c["sw1b"]
                    self.wte(t, "tensor", sw, ("w1ld", b, g_, jc))
                    self.wte(t, "tensor", c["sl2i"], ("l2ld", b, g_))
                    self.wte(t, "tensor", c["as_"], ("relu", b, g_, jc - 2))
                    if g_ == 0 and jc == 0:
                        self.wte(t, "tensor", c["vs"], ("ln2s", b, SC - 1))
                    if jc == 0:
                        self.wte(t, "tensor", c["ts"],
                                 ("proj", b, g_ - 1, 1, HC - 1))
                        for st in range(GSTR):
                            self.wte(t, "tensor", c["vs"],
                                     ("evP", b, g_ - 1, 1, st))
                        if g_ == 0:
                            self.wte(t, "tensor", c["ts"], ("PRJ", b, SC - 1))
                    for dc in range(EC):
                        mm = t.matmul(
                            psF[jc % 2][:, 0:NB],
                            wtile[jc % 2][:, dc * P:(dc + 1) * P],
                            ln2fg[g_ % 2][:, dc * GTOK:dc * GTOK + NB],
                            start=(dc == 0), stop=(dc == EC - 1))
                    c["ts"].inc(mm)
                    self.ev[("fc", b, g_, jc)] = c["ts"].n
                    if jc - 2 >= 0:
                        self.wte(t, "tensor", c["as_"],
                                 ("g2", b, g_, jc - 2))
                        self.wte(t, "tensor", c["scst"], ("consts",))
                        mm = t.matmul(
                            psR[0:1, 0:NB], self.ones[:],
                            g2st[(jc - 2) % 2][:],
                            start=(jc - 2 == 0), stop=(jc - 2 == HC - 1))
                        c["ts"].inc(mm)
                        self.ev[("strow", b, g_, jc - 2)] = c["ts"].n

                @self.blk.scalar
                def _(s):
                    self.wte(s, "scalar", c["ts"], ("fc", b, g_, jc))
                    self.wte(s, "scalar", c["vs"], ("gg", b, g_, jc - 2))
                    c["as_"].inc(s.activation(
                        out=rstage[jc % 2][:], in_=psF[jc % 2][:, 0:NB],
                        func=AF.Relu))
                self.ev[("relu", b, g_, jc)] = c["as_"].n

                @self.blk.vector
                def _(v):
                    self.wte(v, "vector", c["as_"], ("relu", b, g_, jc))
                    c["vs"].inc(v.tensor_mul(
                        self.big0[:, jc * NB:(jc + 1) * NB],
                        rstage[jc % 2][:], rstage[jc % 2][:]))
                self.ev[("gg", b, g_, jc)] = c["vs"].n

                @self.blk.scalar
                def _(s):
                    self.wte(s, "scalar", c["vs"], ("gg", b, g_, jc))
                    self.wte(s, "scalar", c["ts"], ("strow", b, g_, jc - 2))
                    c["as_"].inc(s.activation(
                        out=g2st[jc % 2][:],
                        in_=self.big0[:, jc * NB:(jc + 1) * NB],
                        func=AF.Square))
                self.ev[("g2", b, g_, jc)] = c["as_"].n

                if jc + 2 < HC:
                    w1_load(g_, jc + 2)
                elif jc == HC - 1 and g_ + 1 < NGRP:
                    l2g_load(g_ + 1)

            @self.blk.tensor
            def _(t):
                for jt in (HC - 2, HC - 1):
                    self.wte(t, "tensor", c["as_"], ("g2", b, g_, jt))
                    mm = t.matmul(
                        psR[0:1, 0:NB], self.ones[:], g2st[jt % 2][:],
                        start=(jt == 0), stop=(jt == HC - 1))
                    c["ts"].inc(mm)
                    self.ev[("strow", b, g_, jt)] = c["ts"].n

            @self.blk.vector
            def _(v):
                self.wte(v, "vector", c["ts"], ("strow", b, g_, HC - 1))
                c["vs"].inc(v.tensor_copy(self.ssqrow[0:1, :], psR[0:1, :]))
            self.ev[("ssqrow", b, g_)] = c["vs"].n

            @self.blk.scalar
            def _(s):
                self.wte(s, "scalar", c["vs"], ("ssqrow", b, g_))
                c["as_"].inc(s.activation(
                    out=self.lamrow[0:1, :], in_=self.ssqrow[0:1, :],
                    func=AF.Ln, scale=1.0 / H, bias=self.cbias[0:1, 1:2]))
                self.wt(s, "scalar", c["as_"], c["as_"].n)
                c["as_"].inc(s.activation(
                    out=self.lamrow[0:1, :], in_=self.lamrow[0:1, :],
                    func=AF.Exp, scale=-0.5))
            self.ev[("lrow", b, g_)] = c["as_"].n

            @self.blk.gpsimd
            def _(g):
                self.wte(g, "gpsimd", c["as_"], ("lrow", b, g_))
                dst = bass.AP(self.rowd.tensor, 0, [[1, 1], [1, GTOK]])
                c["srb"].inc(g.dma_start(out=dst, in_=self.lamrow[0:1, :]),
                             16)
            self.ev[("rbo", b, g_)] = c["srb"].n

            @self.blk.sync
            def _(sy):
                self.wte(sy, "sync", c["srb"], ("rbo", b, g_))
                src = bass.AP(self.rowd.tensor, 0, [[1, P], [P, GSTR]])
                c["srb"].inc(sy.dma_start(
                    out=self.lamcol[:], in_=src,
                    allow_slow_non_contiguous=True), 16)
            self.ev[("lam", b, g_)] = c["srb"].n

            if self.phases == "Dfc":
                scF.close()
                break
            scF.close()
            scP = contextlib.ExitStack()
            psP = [scP.enter_context(
                nc.psum_tensor(f"d{b}_{g_}_psP{k}", [P, 1024], F32))
                for k in range(GSTR)]

            def w2_load(dbh, jc):
                @self.blk.sync
                def _(sy):
                    if g_ == 0 and dbh == 0 and jc < 2:
                        self.ldbar(sy, "sync")
                    self.wte(sy, "sync", c["stro"], ("w2T_done",))
                    if jc >= 2:
                        self.wte(sy, "sync", c["ts"],
                                 ("proj", b, g_, dbh, jc - 2))
                    elif dbh == 1:
                        self.wte(sy, "sync", c["ts"],
                                 ("proj", b, g_, 0, HC + jc - 2))
                    else:
                        self.wte(sy, "sync", c["ts"],
                                 ("proj", b, g_ - 1, 1, HC + jc - 2))
                    src = bass.AP(self.w2T_d.tensor, jc * BLK + dbh * 1024,
                                  [[E, P], [1, 1024]])
                    sw = c["sw2a"] if jc % 2 == 0 else c["sw2b"]
                    sw.inc(sy.dma_start(out=w2sl[jc % 2][:], in_=src), 16)
                sw = c["sw2a"] if jc % 2 == 0 else c["sw2b"]
                self.ev[("w2ld", b, g_, dbh, jc)] = sw.n

            for dbh in range(2):
                w2_load(dbh, 0)
                w2_load(dbh, 1)
                for jc in range(HC):
                    @self.blk.tensor
                    def _(t):
                        sw = c["sw2a"] if jc % 2 == 0 else c["sw2b"]
                        self.wte(t, "tensor", sw, ("w2ld", b, g_, dbh, jc))
                        self.wte(t, "tensor", c["as_"],
                                 ("g2", b, g_, HC - 1))
                        if jc == 0:
                            if dbh == 0:
                                self.wte(t, "tensor", c["as_"],
                                         ("relu", b, g_, HC - 1))
                                self.wte(t, "tensor", c["vs"],
                                         ("ssqrow", b, g_))
                            else:
                                for st in range(GSTR):
                                    self.wte(t, "tensor", c["vs"],
                                             ("evP", b, g_, 0, st))
                        for st in range(GSTR):
                            for d2 in range(2):
                                mm = t.matmul(
                                    psP[st][:, d2 * NB:(d2 + 1) * NB],
                                    self.big0[:, jc * NB + st * P:
                                              jc * NB + st * P + P],
                                    w2sl[jc % 2][:, d2 * NB:(d2 + 1) * NB],
                                    start=(jc == 0), stop=(jc == HC - 1))
                        c["ts"].inc(mm)
                        self.ev[("proj", b, g_, dbh, jc)] = c["ts"].n
                    if jc + 2 < HC:
                        w2_load(dbh, jc + 2)

                for st in range(GSTR):
                    @self.blk.vector
                    def _(v):
                        self.wte(v, "vector", c["ts"],
                                 ("proj", b, g_, dbh, HC - 1))
                        if dbh == 0:
                            self.wte(v, "vector", c["sout"],
                                     ("outst", b, g_ - 1, st))
                        c["vs"].inc(v.tensor_copy(
                            projsb[st][:, dbh * 1024:(dbh + 1) * 1024],
                            psP[st][:]))
                    self.ev[("evP", b, g_, dbh, st)] = c["vs"].n

            for st in range(GSTR):
                gi = g_ * GSTR + st

                @self.blk.vector
                def _(v):
                    self.wte(v, "vector", c["vs"], ("evP", b, g_, 1, st))
                    self.wte(v, "vector", c["as_"], ("sqD", b, g_, st - 2))
                    c["vs"].inc(v.reduce_sum(
                        self.col(self.tA, st % 2), projsb[st][:], axis=AX))
                    self.wt(v, "vector", c["vs"], c["vs"].n)
                    c["vs"].inc(v.tensor_scalar_mul(
                        self.col(self.negm3, st % 2),
                        self.col(self.tA, st % 2), -1.0 / E))
                self.ev[("negm3", b, g_, st)] = c["vs"].n

                @self.blk.scalar
                def _(s):
                    self.wte(s, "scalar", c["vs"], ("negm3", b, g_, st))
                    key = (("outst", b, g_, st - 1) if st > 0 else
                           ("outst", b, g_ - 1, GSTR - 1))
                    self.wte(s, "scalar", c["sout"], key)
                    c["as_"].inc(s.activation(
                        out=outstage[:], in_=projsb[st][:], func=AF.Square,
                        bias=self.col(self.negm3, st % 2),
                        accum_out=self.col(self.ssqa, st % 2)))
                self.ev[("sqD", b, g_, st)] = c["as_"].n

                @self.blk.vector
                def _(v):
                    self.wte(v, "vector", c["as_"], ("sqD", b, g_, st))
                    self.wte(v, "vector", c["srb"], ("lam", b, g_))
                    c["vs"].inc(v.tensor_scalar_mul(
                        self.col(self.tA, st % 2),
                        self.col(self.ssqa, st % 2), 1.0 / E))
                    self.wt(v, "vector", c["vs"], c["vs"].n)
                    c["vs"].inc(v.tensor_mul(
                        self.col(self.tB, st % 2),
                        self.lamcol[:, st:st + 1],
                        self.lamcol[:, st:st + 1]))
                    self.wt(v, "vector", c["vs"], c["vs"].n)
                    c["vs"].inc(v.tensor_mul(
                        self.col(self.tA, st % 2),
                        self.col(self.tA, st % 2),
                        self.col(self.tB, st % 2)))
                self.ev[("varD", b, g_, st)] = c["vs"].n

                @self.blk.scalar
                def _(s):
                    self.wte(s, "scalar", c["vs"], ("varD", b, g_, st))
                    c["as_"].inc(s.activation(
                        out=self.col(self.t1c, st % 2),
                        in_=self.col(self.tA, st % 2), func=AF.Ln,
                        bias=self.col(self.cbias, 0)))
                    self.wt(s, "scalar", c["as_"], c["as_"].n)
                    c["as_"].inc(s.activation(
                        out=self.col(self.rstd, st % 2),
                        in_=self.col(self.t1c, st % 2), func=AF.Exp,
                        scale=-0.5))
                self.ev[("rsD", b, g_, st)] = c["as_"].n

                @self.blk.vector
                def _(v):
                    self.wte(v, "vector", c["as_"], ("rsD", b, g_, st))
                    c["vs"].inc(v.tensor_mul(
                        self.col(self.lam, st % 2),
                        self.col(self.rstd, st % 2),
                        self.lamcol[:, st:st + 1]))
                    self.wt(v, "vector", c["vs"], c["vs"].n)
                    c["vs"].inc(v.tensor_scalar(
                        outstage[:], projsb[st][:],
                        self.col(self.negm3, st % 2),
                        self.col(self.lam, st % 2),
                        op0=ALU.add, op1=ALU.mult))
                self.ev[("outpre", b, g_, st)] = c["vs"].n

                @self.blk.gpsimd
                def _(g):
                    self.wte(g, "gpsimd", c["vs"], ("outpre", b, g_, st))
                    src = bass.AP(self.ys.tensor, gi * BLK, [[E, P], [1, E]])
                    c["sya"].inc(g.dma_start(
                        out=outstage[:], in_=src, accum_op=ALU.add), 16)
                    self.ev[("yacc", b, g_, st)] = c["sya"].n
                    self.wt(g, "gpsimd", c["sya"], c["sya"].n)
                    c["sout"].inc(g.dma_start(
                        out=self.out[b, gi * P:(gi + 1) * P, :],
                        in_=outstage[:]), 16)
                self.ev[("outst", b, g_, st)] = c["sout"].n

            scP.close()
            if self.phases == "Dg0":
                break
            if self.phases == "Dg1" and g_ == 1:
                break
            if self.phases == "Dg2" and g_ == 2:
                break
            if g_ == NGRP - 1 and b + 1 < NBATCH:
                @self.blk.gpsimd
                def _(g):
                    self.wte(g, "gpsimd", c["ts"], ("PRJ", b, SC - 1))
                    for cch in range(EC):
                        c["swq"].inc(g.dma_start(
                            out=self.big2[:, cch * E:(cch + 1) * E],
                            in_=self.wq[cch * P:(cch + 1) * P, :]), 16)
                self.ev[("wqld", b + 1)] = c["swq"].n
        self.ev[("outdone", b)] = c["sout"].n
        sc.close()

    def finale(self):
        c = self.c

        @self.blk.sync
        def _(sy):
            for n in ("sout", "syo", "sqo0", "sqo1", "svo0", "svo1",
                      "sqs0", "sqs1", "stro", "sl2o", "sya", "sxc", "srb"):
                self.wt(sy, "sync", c[n], c[n].n)

        if self.debug:
            @self.blk.sync
            def _(sy):
                for n, v in self.barrier_vals().items():
                    self.wt(sy, "sync", c[n], v)
                sy.dma_start(out=self.dbg0, in_=self.big0[:]).then_inc(
                    c["sout"].h, 16)
                sy.dma_start(out=self.dbg2, in_=self.big2[:]).then_inc(
                    c["sout"].h, 16)
                sy.wait_ge(c["sout"].h, c["sout"].n + 32)


_NC_CACHE = {}


def _get_nc(debug=None, phases=None):
    key = (debug, phases)
    if key not in _NC_CACHE:
        _NC_CACHE[key] = MK(debug=debug, phases=phases).build()
    return _NC_CACHE[key]


def _in_maps(x, wq, wk, wv, wo, wfc, wproj, ncores=8):
    x = np.ascontiguousarray(np.asarray(x, dtype=np.float32))
    ident = np.eye(P, dtype=ml_dtypes.bfloat16)
    ones = np.ones((P, 1), dtype=ml_dtypes.bfloat16)
    cbias = np.tile(np.array([[EPS_LN, EPS_RMS]], dtype=np.float32), (P, 1))
    mk = lambda a: np.ascontiguousarray(np.asarray(a, dtype=np.float32))
    maps = []
    for cidx in range(ncores):
        maps.append({
            "x": x[cidx * NBATCH:(cidx + 1) * NBATCH],
            "wq": mk(wq), "wk": mk(wk), "wv": mk(wv), "wo": mk(wo),
            "wfc": mk(wfc), "wproj": mk(wproj),
            "ident_in": ident, "ones_in": ones, "cbias_in": cbias,
        })
    return maps


def kernel(L0_attn_x, L0_attn_Wq, L0_attn_Wk, L0_attn_Wv, L0_attn_Wo,
           L0_ffn_W_fc, L0_ffn_W_proj):
    nc = _get_nc()
    maps = _in_maps(L0_attn_x, L0_attn_Wq, L0_attn_Wk, L0_attn_Wv,
                    L0_attn_Wo, L0_ffn_W_fc, L0_ffn_W_proj)
    res = run_bass_kernel_spmd(nc, maps, core_ids=list(range(8)))
    out = np.concatenate([np.asarray(r["out"]) for r in res.results], axis=0)
    return out.astype(np.float32)
